# version 22
# baseline (speedup 1.0000x reference)
import contextlib
import numpy as np
import ml_dtypes
import concourse.bass as bass
import concourse.mybir as mybir
from concourse.bass_utils import run_bass_kernel_spmd
from concourse.alu_op_type import AluOpType as OP

F32 = mybir.dt.float32
BF16 = mybir.dt.bfloat16
AF = mybir.ActivationFunctionType

D = 1024
TPC = 4096
XR = TPC + 128
NSTI = 8
NCTX = 640
EPS_LN = 1e-5
EPS_RMS = 1e-6
ROPE_BASE = 10000.0
PIECE = 8192
PHASE = 99
NOMASS = False
NOEVEN = False
DUMP = None
PROF_REGIONS = False
REGIONS = []


class _Stop(Exception):
    pass

_BP = [("wq", 0, 8192), ("wk", 8192, 8192), ("wv", 16384, 8192),
       ("wo", 24576, 8192)]
for _i in range(4):
    _BP.append((f"rw1_{_i}", 36864 + 8192 * _i, 8192))
for _i in range(4):
    _BP.append((f"rw2_{_i}", 69632 + 8192 * _i, 8192))
BLOB_COLS = 69632 + 8192 * 4


def _host_consts(p0):
    half = 32
    theta = 1.0 / ROPE_BASE ** (np.arange(half, dtype=np.float64) / half)
    pos = np.clip(p0 + np.arange(XR) - 128, 0, None).astype(np.float64)
    fr = pos[None, :] * theta[:, None]
    cct = np.tile(np.cos(fr), (4, 1)).astype(ml_dtypes.bfloat16)
    sst = np.tile(np.sin(fr), (4, 1)).astype(ml_dtypes.bfloat16)

    c = np.arange(128)[:, None]
    j = np.arange(64)[None, :]
    band = ((c >= j) & (c <= j + 63)).astype(np.float32)
    m_main = np.concatenate([band, band], axis=1)
    bf = band * (c >= 64) if p0 == 0 else band
    m_first = np.concatenate([bf, bf], axis=1)

    P = np.zeros((128, 128), np.float32)
    for m in range(128):
        g, r = m // 64 * 64, m % 64
        if r < 32:
            P[g + r + 32, m] = -1.0
        else:
            P[g + r - 32, m] = 1.0
    return dict(
        cct=cct, sstb=sst,
        mask_main=np.tile(m_main, (1, 8)).astype(ml_dtypes.bfloat16),
        mask_first=np.tile(m_first, (1, 8)).astype(ml_dtypes.bfloat16),
        prot=P.astype(ml_dtypes.bfloat16),
        eye=np.eye(128, dtype=ml_dtypes.bfloat16),
        onesmat=np.ones((128, 128), dtype=ml_dtypes.bfloat16),
        onesrow=np.ones((1, 128), dtype=ml_dtypes.bfloat16),
        onescol=np.ones((128, 1), dtype=ml_dtypes.bfloat16),
    )


class Sem:
    def __init__(self, stack, nc, name):
        self.h = stack.enter_context(nc.semaphore(name))
        self.n = 0


class Eng:
    def __init__(self, e, sem, step=1):
        self.e = e
        self.sem = sem
        self.step = step
        self.waited = {}

    def wait(self, tok):
        if tok is None:
            return
        sem, v = tok
        if self.waited.get(id(sem), -1) >= v:
            return
        self.e.wait_ge(sem.h, v)
        self.waited[id(sem)] = v

    def fence(self, instr):
        instr.then_inc(self.sem.h, self.step)
        self.sem.n += self.step
        return (self.sem, self.sem.n)


class Buf:
    __slots__ = ("w", "rs", "name", "aliases")

    def __init__(self, name=""):
        self.w = {}
        self.rs = {}
        self.name = name
        self.aliases = []

    @staticmethod
    def _add(d, tok):
        cur = d.get(id(tok[0]))
        if cur is None or tok[1] > cur[1]:
            d[id(tok[0])] = tok

    def add_r(self, tok):
        self._add(self.rs, tok)

    def add_w(self, tok):
        self._add(self.w, tok)


def pre_write(E, bufs):
    for b in bufs:
        for t in b.w.values():
            E.wait(t)
        for t in b.rs.values():
            E.wait(t)
        for a in b.aliases:
            for t in a.w.values():
                E.wait(t)
            for t in a.rs.values():
                E.wait(t)


def pre_read(E, bufs):
    for b in bufs:
        for t in b.w.values():
            E.wait(t)


def selfedge(E, instr):
    E.wait(E.fence(instr))


def post(E, instr, writes=(), reads=()):
    tok = E.fence(instr)
    for b in writes:
        b.w = {id(tok[0]): tok}
        b.rs = {}
    for b in reads:
        b.add_r(tok)
    return tok


class DQueue:

    def __init__(self, e, lanes):
        self.e = e
        self.lanes = lanes
        self.i = 0
        self.waited = {}

    def wait(self, tok):
        if tok is None:
            return
        sem, v = tok
        if self.waited.get(id(sem), -1) >= v:
            return
        self.e.wait_ge(sem.h, v)
        self.waited[id(sem)] = v

    def dma(self, out, in_, writes=(), reads=()):
        ln = self.lanes[self.i % len(self.lanes)]
        self.i += 1
        self.wait((ln, ln.n))
        for b in writes:
            pre_write(self, [b])
        for b in reads:
            pre_read(self, [b])
        instr = self.e.dma_start(out=out, in_=in_)
        instr.then_inc(ln.h, 16)
        ln.n += 16
        tok = (ln, ln.n)
        for b in writes:
            b.w = {id(ln): tok}
            b.rs = {}
        for b in reads:
            b.add_r(tok)
        return tok

    def dma_add(self, out, in_, writes=(), reads=()):
        ln = self.lanes[self.i % len(self.lanes)]
        self.i += 1
        self.wait((ln, ln.n))
        for b in writes:
            pre_write(self, [b])
        for b in reads:
            pre_read(self, [b])
        instr = self.e.dma_start(out=out, in_=in_)
        instr.then_inc(ln.h, 16)
        ln.n += 16
        tok = (ln, ln.n)
        for b in writes:
            b.add_w(tok)
        for b in reads:
            b.add_r(tok)
        return tok


def build_program():
    stack = contextlib.ExitStack()
    nc = bass.Bass("TRN2", target_bir_lowering=False)

    def din(name, shape, dt):
        return nc.dram_tensor(name, list(shape), dt, kind="ExternalInput").ap()

    X = din("x", [XR, D], F32)
    Wq = din("wq", [128, 8192], F32); Wk = din("wk", [128, 8192], F32)
    Wv = din("wv", [128, 8192], F32); Wo = din("wo", [128, 8192], F32)
    SW1 = din("sw1", [128, 4096], F32); SW2 = din("sw2", [128, 4096], F32)
    RW1 = din("rw1", [8, 128, 4096], F32); RW2 = din("rw2", [8, 128, 4096], F32)
    RoutT = din("routingT", [128, 64], F32)
    Rb2Sb2 = din("rb2sb2", [9, D], F32)
    EBias = din("ebias", [1, 8], F32)
    Rb1c = din("rb1c", [128, 32], F32)
    Sb1c = din("sb1c", [128, 4], F32)
    ScaleBC = din("scale_bc", [128, D], F32)
    CCT = din("cct", [128, XR], BF16); SSTd = din("sstb", [128, XR], BF16)
    MaskM = din("mask_main", [128, 1024], BF16)
    MaskF = din("mask_first", [128, 1024], BF16)
    ProtD = din("prot", [128, 128], BF16)
    EyeD = din("eye", [128, 128], BF16)
    OnesMD = din("onesmat", [128, 128], BF16)
    OnesRD = din("onesrow", [1, 128], BF16)
    OnesCD = din("onescol", [128, 1], BF16)
    OUT = nc.dram_tensor("y", [TPC, D], F32, kind="ExternalOutput").ap()
    DBG = nc.dram_tensor("dbg", [128, 8192], F32, kind="ExternalOutput").ap() \
        if DUMP is not None else None
    BLOB = nc.dram_tensor("wblob", [128, BLOB_COLS], BF16, kind="Internal").ap()

    def sbt(name, shape, dt):
        return stack.enter_context(nc.sbuf_tensor(name, list(shape), dt))

    ring = sbt("ring", [128, 2 * PIECE], BF16)
    c_sw2 = sbt("c_sw2", [128, 4096], BF16)
    c_sw1 = sbt("c_sw1", [128, 4096], BF16)
    cc_s = sbt("cc_s", [128, 2 * NCTX], BF16)
    ss_s = sbt("ss_s", [128, 2 * NCTX], BF16)
    x_sb = sbt("x_sb", [128, 5 * D], F32)
    xln = sbt("xln", [128, 2 * D], BF16)
    arn = sbt("arn", [128, 36608], BF16)
    z_sb = sbt("z_sb", [128, 4 * D], F32)
    y1 = sbt("y1", [128, 4 * D], F32)
    trash = sbt("trash", [128, D], F32)
    c_mm = sbt("c_mm", [128, 1024], BF16)
    c_mf = sbt("c_mf", [128, 1024], BF16)
    c_eye = sbt("c_eye", [128, 128], BF16)
    c_ones = sbt("c_ones", [128, 128], BF16)
    c_prot = sbt("c_prot", [128, 128], BF16)
    c_or = sbt("c_or", [1, 128], BF16)
    c_oc = sbt("c_oc", [128, 1], BF16)
    c_scale = sbt("c_scale", [128, D], F32)
    c_rb1 = sbt("c_rb1", [128, 32], F32)
    c_sb1 = sbt("c_sb1", [128, 4], F32)
    c_rt = sbt("c_rt", [128, 64], BF16)
    c_rb2 = sbt("c_rb2", [9, D], BF16)
    c_eb = sbt("c_eb", [1, 8], BF16)
    c_im = sbt("c_im", [128, 8], F32)
    mgA = sbt("mgA", [128, 8], F32)
    mgB = sbt("mgB", [128, 8], F32)
    mgC = sbt("mgC", [128, 8], F32)
    st6 = sbt("st6", [128, 12], F32)
    st2 = sbt("st2", [128, 2], F32)
    stat = sbt("stat", [128, 5 * 4], F32)
    zss = sbt("zss", [128, 4], F32)
    zrs = sbt("zrs", [128, 4], F32)
    rmas = sbt("rmas", [128, 16], F32)
    rm2 = sbt("rm2", [128, 16], F32)
    logi = sbt("logi", [128, 32], F32)
    rsc1 = sbt("rsc1", [128, 32], F32)
    rsc2 = sbt("rsc2", [128, 32], F32)
    rsc3 = sbt("rsc3", [128, 2], F32)
    cf_sb = sbt("cf_sb", [128, 9 * 4], F32)
    cfb = sbt("cfb", [128, 9 * 4], BF16)
    cfT = sbt("cfT", [9, 4 * 128], BF16)

    off = 0
    def _take(n):
        nonlocal off
        o = off; off += n; return o
    A_lnxT = _take(8 * NCTX)
    A_qro = _take(8 * 512)
    A_kro = _take(8 * NCTX)
    A_v = _take(5 * 1024)
    A_v64 = _take(4 * 1024)
    A_qd = _take(2 * 1024)
    A_tau = _take(2 * 1024)
    A_mtk = _take(2 * 1024)
    A_dg = _take(16 * 128)
    A_msgT = _take(8 * 128)
    A_qtp = _take(2 * NCTX)
    A_t1 = _take(2 * NCTX)
    A_t2 = _take(2 * NCTX)
    assert off <= 36608, off
    ATTN_REG = [(["lnxT"], A_lnxT, 5120), (["qro"], A_qro, 4096),
                (["kro"], A_kro, 5120), (["vv"], A_v, 5120),
                (["v64"], A_v64, 4096), (["qd"], A_qd, 2048),
                (["tau0", "tau1"], A_tau, 2048), (["mtk"], A_mtk, 2048),
                (["dg"], A_dg, 2048), (["msgT"], A_msgT, 1024),
                (["qtp0", "qtp1"], A_qtp, 1280), (["t1_0", "t1_1"], A_t1, 1280),
                (["t2_0", "t2_1"], A_t2, 1280)]
    off2 = 0
    def _take2(n):
        nonlocal off2
        o = off2; off2 += n; return o
    F_ln2T = _take2(8 * 512)
    F_sh = _take2(4 * 512)
    F_g = [_take2(4 * 512) for _ in range(8)]
    F_cbc = _take2(512)
    F_dcf = _take2(4 * 128)
    FFN_REG = [(["ln2T"], F_ln2T, 4096), (["sh"], F_sh, 2048)] + \
              [([f"g{e}"], F_g[e], 2048) for e in range(8)] + \
              [(["cbc"], F_cbc, 512), (["dcf"], F_dcf, 512)]
    assert off2 <= off

    bufs = {}
    def B(name):
        if name not in bufs:
            bufs[name] = Buf(name)
        return bufs[name]

    for ans, ao, al in ATTN_REG:
        for fns, fo, fl in FFN_REG:
            if ao < fo + fl and fo < ao + al:
                for an in ans:
                    for fn in fns:
                        B(an).aliases.append(B(fn))
                        B(fn).aliases.append(B(an))

    def ps(name, shape, dt):
        return stack.enter_context(nc.psum_tensor(name, list(shape), dt))
    pA = ps("pA", [128, 512], F32); pB = ps("pB", [128, 512], F32)
    pC = ps("pC", [128, 512], F32); pD = ps("pD", [128, 512], F32)
    pE = ps("pE", [128, 512], F32); pF = ps("pF", [128, 512], F32)
    pTR = ps("pTR", [128, 1024], BF16)
    pM = ps("pM", [128, 512], F32)
    PB = {id(pA): B("pA"), id(pB): B("pB"), id(pC): B("pC"), id(pD): B("pD"),
          id(pE): B("pE"), id(pF): B("pF"), id(pTR): B("pTR"), id(pM): B("pM")}
    def pb(t):
        return PB[id(t)]

    qW = DQueue(nc.sync, [Sem(stack, nc, f"qW{i}") for i in range(4)])
    qX = DQueue(nc.gpsimd, [Sem(stack, nc, f"qX{i}") for i in range(4)])
    qO = DQueue(nc.scalar, [Sem(stack, nc, f"qO{i}") for i in range(2)])
    PE = Eng(nc.tensor, Sem(stack, nc, "sP"))
    V = Eng(nc.vector, Sem(stack, nc, "sV"))
    A = Eng(nc.scalar, Sem(stack, nc, "sA"))
    G = Eng(nc.gpsimd, Sem(stack, nc, "sG"))

    for dst, src, nm in [
            (c_mm[:], MaskM, "c_mm"), (c_mf[:], MaskF, "c_mf"),
            (c_eye[:], EyeD, "c_eye"), (c_ones[:], OnesMD, "c_ones"),
            (c_prot[:], ProtD, "c_prot"), (c_or[:], OnesRD, "c_or"),
            (c_oc[:], OnesCD, "c_oc"), (c_scale[:], ScaleBC, "c_scale"),
            (c_rb1[:], Rb1c, "c_rb1"), (c_sb1[:], Sb1c, "c_sb1"),
            (c_rb2[:], Rb2Sb2, "c_rb2"), (c_eb[:], EBias, "c_eb")]:
        qX.dma(dst, src, writes=[B(nm)])
    qX.dma(c_rt[:], RoutT, writes=[B("c_rt")])
    qX.dma(c_sw2[:], SW2, writes=[B("c_sw2")])
    qX.dma(c_sw1[:], SW1, writes=[B("c_sw1")])

    i = nc.vector.memset(arn[:, A_qd:A_qd + 2048], 0.0)
    post(V, i, writes=[B("qd")])
    i = nc.vector.memset(cf_sb[:], 1.0)
    post(V, i, writes=[B("cf")])
    for _c, _v in enumerate([EPS_LN, 1.0, 1.0 / D, EPS_RMS, -0.5, -1.0, 0.5, 0.125]):
        i = nc.vector.memset(c_im[:, _c:_c + 1], _v)
    post(V, i, writes=[B("c_im")])
    A.wait((V.sem, V.sem.n))

    CAST = {"wq": [(0, Wq)], "wk": [(0, Wk)], "wv": [(0, Wv)], "wo": [(0, Wo)]}
    for _e in range(4):
        CAST[f"rw1_{_e}"] = [(0, RW1[2 * _e]), (4096, RW1[2 * _e + 1])]
        CAST[f"rw2_{_e}"] = [(0, RW2[2 * _e]), (4096, RW2[2 * _e + 1])]
    slotbuf = [B("slot0"), B("slot1")]
    blobbuf = [B(f"blob{pi}") for pi in range(len(_BP))]

    def emit_casts():
        prev = None
        for pi, (nm, boff, ncols) in enumerate(_BP):
            csem = Sem(stack, nc, f"cs{pi}")
            if pi <= 4 and prev is not None:
                nc.gpsimd.wait_ge(prev.h, prev.n)
            for (off_, srcap) in CAST[nm]:
                w = srcap.shape[-1]
                i = nc.gpsimd.dma_start(out=BLOB[:, boff + off_:boff + off_ + w],
                                        in_=srcap)
                i.then_inc(csem.h, 16)
                csem.n += 16
            blobbuf[pi].w = {id(csem): (csem, csem.n)}
            prev = csem

    def mm(out_ap, lhsT, rhs, start, stop):
        return nc.tensor.matmul(out_ap, lhsT, rhs, start=start, stop=stop)

    def _reg(tag):
        if PROF_REGIONS:
            REGIONS.append((tag, int(nc.get_next_instruction_name().split("-")[1])))

    from concourse.hw_specs import get_activation_tables
    _tabs = list(get_activation_tables(nc.m.arch).values())

    def _set_id(funcs):
        for _i, _fs in enumerate(_tabs):
            if all(f in _fs for f in funcs):
                return _i
        raise ValueError(funcs)

    SET_IDS = {
        "sigmoid": _set_id([AF.Sigmoid, AF.Square, AF.Copy]),
        "nlexp": _set_id([AF.Ln, AF.Exp, AF.Abs, AF.Relu, AF.Square, AF.Copy]),
        "silu": _set_id([AF.Silu, AF.Square, AF.Copy]),
    }
    cur_set = [None]

    def use_set(name):
        if cur_set[0] != name:
            nc.scalar.add_instruction(mybir.InstLoadActFuncSet(
                name=nc.get_next_instruction_name(),
                act_func_set_id=SET_IDS[name], ins=[], outs=[]))
            cur_set[0] = name

    lseq = [0]
    def load_piece(pi):
        sl = lseq[0] % 2
        lseq[0] += 1
        sbf = slotbuf[sl]
        nm, boff, ncols = _BP[pi]
        qW.dma(ring[:, sl * PIECE:sl * PIECE + ncols], BLOB[:, boff:boff + ncols],
               writes=[sbf], reads=[blobbuf[pi]])
        return sl * PIECE, sbf

    MAGIC = 0x5f3759df

    def rsqrt_magic(dst, pre, n):
        y = mgB[:, 0:n]
        h = mgC[:, 0:n]
        pre_read(V, [B("mg")])
        selfedge(V, nc.vector.tensor_scalar(y.bitcast(mybir.dt.int32),
                                            pre.bitcast(mybir.dt.int32), 1, 0,
                                            OP.logical_shift_right,
                                            OP.logical_shift_right))
        selfedge(V, nc.vector.tensor_scalar(y.bitcast(mybir.dt.int32),
                                            y.bitcast(mybir.dt.int32), -1, MAGIC,
                                            OP.mult, OP.add))
        for _ in range(2):
            selfedge(V, nc.vector.tensor_tensor(h, y, y, OP.mult))
            selfedge(V, nc.vector.tensor_tensor(h, h, pre, OP.mult))
            selfedge(V, nc.vector.tensor_scalar(h, h, -0.5, 1.5, OP.mult, OP.add))
            selfedge(V, nc.vector.tensor_tensor(y, y, h, OP.mult))
        i = nc.vector.tensor_copy(dst, y)
        return i

    def ckpt(n):
        if PHASE <= n:
            raise _Stop()

    def tap(tag, ap, bufs, cols):
        if DUMP != tag:
            return
        qX.dma(DBG[:, 0:cols], ap, reads=bufs)
        raise _Stop()

    try:
        def prologue_loads(s):
            r0 = 512 * s
            sso = (s % 2) * NCTX
            xbufs = [B(f"x{ct}") for ct in range(5)]
            for ct in range(5):
                qX.dma(x_sb[:, ct * D:(ct + 1) * D],
                       X[r0 + 128 * ct:r0 + 128 * ct + 128, :], writes=[xbufs[ct]])
            ccb = B("cc")
            qX.dma(cc_s[:, sso:sso + NCTX], CCT[:, r0:r0 + NCTX], writes=[ccb])
            qX.dma_add(ss_s[:, sso:sso + NCTX], SSTd[:, r0:r0 + NCTX], writes=[ccb])
            return xbufs, ccb

        def prologue(s, pre=None):
            if pre is None:
                pre = prologue_loads(s)
            xbufs, ccb = pre
            lnxTb = B("lnxT")
            for ct in range(5):
                xb = xbufs[ct]
                pre_read(V, [xb])
                pre_write(V, [B("st6"), B("st2"), B("stat"), B("mg")])
                nc.vector.bn_stats(st6[:, 0:6], x_sb[:, ct * D:ct * D + 512])
                selfedge(V, nc.vector.bn_stats(st6[:, 6:12],
                                               x_sb[:, ct * D + 512:ct * D + 1024]))
                selfedge(V, nc.vector.bn_aggr(st2[:], st6[:]))
                nc.vector.tensor_copy(stat[:, 4 * ct:4 * ct + 1], st2[:, 0:1])
                i = nc.vector.tensor_scalar(mgA[:, ct:ct + 1], st2[:, 1:2],
                                            1.0, EPS_LN, OP.mult, OP.add)
                post(V, i, writes=[B("st6"), B("st2"), B("stat"), B("mg")], reads=[xb])
            pre_write(V, [B("stat"), B("mg")])
            selfedge(V, rsqrt_magic(
                stat[:].rearrange("p (c f) -> p c f", f=4)[:, :, 1:2], mgA[:, 0:5], 5))
            selfedge(V, nc.vector.tensor_tensor(
                stat[:].rearrange("p (c f) -> p c f", f=4)[:, :, 2:3],
                stat[:].rearrange("p (c f) -> p c f", f=4)[:, :, 0:1],
                stat[:].rearrange("p (c f) -> p c f", f=4)[:, :, 1:2], OP.mult))
            i = nc.vector.tensor_scalar(
                stat[:].rearrange("p (c f) -> p c f", f=4)[:, :, 2:3],
                stat[:].rearrange("p (c f) -> p c f", f=4)[:, :, 2:3],
                -1.0, 0.0, OP.mult, OP.add)
            post(V, i, writes=[B("stat"), B("mg")])
            xslot = [(xln, 0, B("xln0")), (xln, D, B("xln1")),
                     (arn, A_tau, B("tau0")), (arn, A_tau + D, B("tau1")),
                     (arn, A_mtk, B("mtk"))]
            for ct in range(5):
                t, xo, xlb = xslot[ct]
                pre_read(V, [xbufs[ct], B("stat")])
                pre_write(V, [xlb])
                i = nc.vector.tensor_scalar(t[:, xo:xo + D], x_sb[:, ct * D:(ct + 1) * D],
                                            stat[:, 4 * ct + 1:4 * ct + 2],
                                            stat[:, 4 * ct + 2:4 * ct + 3],
                                            OP.mult, OP.add)
                post(V, i, writes=[xlb], reads=[xbufs[ct], B("stat")])
            for ct in range(5):
                t, xo, xlb = xslot[ct]
                pre_read(PE, [xlb, B("c_eye")])
                pre_write(PE, [pb(pTR)])
                for ch in range(8):
                    i = nc.tensor.transpose(pTR[:, 128 * ch:128 * ch + 128],
                                            t[:, xo + 128 * ch:xo + 128 * ch + 128],
                                            c_eye[:])
                post(PE, i, writes=[pb(pTR)], reads=[xlb, B("c_eye")])
                pre_read(V, [pb(pTR)])
                pre_write(V, [lnxTb])
                i = nc.vector.tensor_copy(
                    arn[:, A_lnxT:A_lnxT + 8 * NCTX].rearrange(
                        "p (k c) -> p k c", k=8)[:, :, 128 * ct:128 * ct + 128],
                    pTR[:].rearrange("p (k c) -> p k c", k=8))
                post(V, i, writes=[lnxTb], reads=[pb(pTR)])
            return xbufs, ccb, lnxTb

        _reg("init")
        _pre0 = prologue_loads(0)
        emit_casts()
        STV = {}
        STV[0] = prologue(0, _pre0)
        for s in range(NSTI):
            r0 = 512 * s
            sso = (s % 2) * NCTX

            _reg(f"s{s}:led")
            wqo, wqb = load_piece(0)
            wko, wkb = load_piece(1)
            xbufs, ccb, lnxTb = STV[s]

            tap("lnxT", arn[:, A_lnxT:A_lnxT + 5120], [lnxTb], 5120)
            tap("stat", stat[:].rearrange("p (c f) -> p c f", f=4)[:, :, 1:3], [B("stat")], 10)
            ckpt(2)
            qrob, krob, vb = B("qro"), B("kro"), B("vv")

            def stageA(ot):
                B0 = 8 * s + 2 * ot
                qdb = B("qd")
                for blk in range(2):
                    Bg = B0 + blk
                    qdo = A_qd + 1024 * blk
                    toko = 64 * (Bg - 8 * s - 2)
                    qW.dma_add(
                        arn[0:64, qdo:qdo + 1024].rearrange(
                            "p (g c) -> p g c", g=8)[:, :, 0:64],
                        arn[0:64, A_qro:A_qro + 4096].rearrange(
                            "p (g c) -> p g c", g=8)[:, :, toko:toko + 64],
                        writes=[qdb], reads=[qrob])
                    qW.dma_add(
                        arn[64:128, qdo:qdo + 1024].rearrange(
                            "p (g c) -> p g c", g=8)[:, :, 64:128],
                        arn[64:128, A_qro:A_qro + 4096].rearrange(
                            "p (g c) -> p g c", g=8)[:, :, toko:toko + 64],
                        writes=[qdb], reads=[qrob])

            def stageB_mms(ot, blk):
                Bg = 8 * s + 2 * ot + blk
                c0 = 64 * (Bg - 1 - 8 * s)
                pl, ph = (pC, pD) if blk == 0 else (pE, pF)
                qdo = A_qd + 1024 * blk
                pre_read(PE, [krob, B("qd")])
                pre_write(PE, [pb(pl), pb(ph)])
                for p in range(8):
                    rh = arn[:, qdo + 128 * p:qdo + 128 * p + 128]
                    pt = pl if p < 4 else ph
                    oc = 128 * (p % 4)
                    lt = arn[:, A_kro + p * NCTX + c0:A_kro + p * NCTX + c0 + 128]
                    i = mm(pt[:, oc:oc + 128], lt, rh, True, True)
                post(PE, i, writes=[pb(pl), pb(ph)], reads=[krob, B("qd")])

            def stageB_act(ot, blk):
                Bg = 8 * s + 2 * ot + blk
                pl, ph = (pC, pD) if blk == 0 else (pE, pF)
                tauo = A_tau + 1024 * blk
                taub = B(f"tau{blk}")
                pre_read(A, [pb(pl), pb(ph)])
                pre_write(A, [taub])
                nc.scalar.activation(arn[:, tauo:tauo + 512], pl[:],
                                     AF.Sigmoid, scale=c_im[:, 7:8])
                i = nc.scalar.activation(arn[:, tauo + 512:tauo + 1024], ph[:],
                                         AF.Sigmoid, scale=c_im[:, 7:8])
                post(A, i, writes=[taub], reads=[pb(pl), pb(ph)])
                mk, mkb = (c_mf, B("c_mf")) if Bg == 2 else (c_mm, B("c_mm"))
                pre_read(G, [taub, mkb])
                pre_write(G, [taub])
                i = nc.gpsimd.tensor_tensor(arn[:, tauo:tauo + 1024],
                                            arn[:, tauo:tauo + 1024], mk[:],
                                            OP.mult)
                post(G, i, writes=[taub], reads=[mkb])

            def stageB(ot, blks=(0, 1)):
                for blk in blks:
                    stageB_mms(ot, blk)
                    stageB_act(ot, blk)

            def proj_rope(kind, wo_, wb_, dsto, dstb, fillers=()):
                ncols = 512 if kind == "q" else 640
                base = 128 if kind == "q" else 0

                qtp_offs = (A_qtp, A_qtp + NCTX, A_dg)

                def stage1(ch):
                    acc = pA if ch % 2 == 0 else pC
                    tail = ncols == 640
                    pre_read(PE, [lnxTb, wb_])
                    pre_write(PE, [pb(acc)] + ([pb(pM)] if tail else []))
                    for k in range(8):
                        lt = ring[:, wo_ + k * 1024 + 128 * ch:wo_ + k * 1024 + 128 * ch + 128]
                        rh = arn[:, A_lnxT + k * NCTX + base:A_lnxT + k * NCTX + base + 512]
                        i = mm(acc[:], lt, rh, k == 0, k == 7)
                    if tail:
                        for k in range(8):
                            lt = ring[:, wo_ + k * 1024 + 128 * ch:wo_ + k * 1024 + 128 * ch + 128]
                            rh = arn[:, A_lnxT + k * NCTX + 512:A_lnxT + k * NCTX + 640]
                            i = mm(pM[:, 0:128] if ch % 2 == 0 else pM[:, 256:384],
                                   lt, rh, k == 0, k == 7)
                    post(PE, i, writes=[pb(acc)] + ([pb(pM)] if tail else []),
                         reads=[lnxTb, wb_])
                    qtp_o = qtp_offs[ch % 3]
                    qtb = B(f"qtp{ch % 3}")
                    pre_read(A, [pb(acc)] + ([pb(pM)] if tail else []))
                    pre_write(A, [qtb])
                    i = nc.scalar.copy(arn[:, qtp_o:qtp_o + 512], acc[:])
                    if tail:
                        i = nc.scalar.copy(arn[:, qtp_o + 512:qtp_o + 640],
                                           pM[:, 0:128] if ch % 2 == 0 else pM[:, 256:384])
                    post(A, i, writes=[qtb],
                         reads=[pb(acc)] + ([pb(pM)] if tail else []))

                def stage2(ch):
                    tail = ncols == 640
                    rot = pB if ch % 2 == 0 else pD
                    qtp_o = qtp_offs[ch % 3]
                    t1o = A_t1 + (ch % 2) * NCTX
                    t2o = A_t2 + (ch % 2) * NCTX
                    qtb = B(f"qtp{ch % 3}")
                    t1b = B(f"t1_{ch % 2}")
                    t2b = B(f"t2_{ch % 2}")
                    pre_read(PE, [qtb, B("c_prot")])
                    pre_write(PE, [pb(rot)] + ([pb(pM)] if tail else []))
                    i = mm(rot[:], c_prot[:], arn[:, qtp_o:qtp_o + 512], True, True)
                    if tail:
                        i = mm(pM[:, 128:256] if ch % 2 == 0 else pM[:, 384:512],
                               c_prot[:], arn[:, qtp_o + 512:qtp_o + 640], True, True)
                    post(PE, i, writes=[pb(rot)] + ([pb(pM)] if tail else []),
                         reads=[qtb, B("c_prot")])
                    pre_read(V, [qtb, ccb])
                    pre_write(V, [t1b])
                    i = nc.vector.tensor_tensor(
                        arn[:, t1o:t1o + ncols], arn[:, qtp_o:qtp_o + ncols],
                        cc_s[:, sso + base:sso + base + ncols], OP.mult)
                    post(V, i, writes=[t1b], reads=[qtb, ccb])
                    pre_read(V, [pb(rot), ccb] + ([pb(pM)] if tail else []))
                    pre_write(V, [t2b])
                    i = nc.vector.tensor_tensor(arn[:, t2o:t2o + 512], rot[:],
                                                ss_s[:, sso + base:sso + base + 512], OP.mult)
                    if tail:
                        i = nc.vector.tensor_tensor(
                            arn[:, t2o + 512:t2o + 640],
                            pM[:, 128:256] if ch % 2 == 0 else pM[:, 384:512],
                            ss_s[:, sso + base + 512:sso + base + 640], OP.mult)
                    post(V, i, writes=[t2b],
                         reads=[pb(rot), ccb] + ([pb(pM)] if tail else []))
                    pre_read(V, [t1b, t2b])
                    pre_write(V, [dstb])
                    i = nc.vector.tensor_tensor(
                        arn[:, dsto + ch * ncols:dsto + (ch + 1) * ncols],
                        arn[:, t1o:t1o + ncols], arn[:, t2o:t2o + ncols], OP.add)
                    post(V, i, writes=[dstb], reads=[t1b, t2b])

                fill = list(fillers)
                for ch in range(8):
                    stage1(ch)
                    if ch >= 2:
                        stage2(ch - 2)
                        if fill:
                            fill.pop(0)()
                stage2(6)
                stage2(7)
                while fill:
                    fill.pop(0)()

            _reg(f"s{s}:qkv_q")
            proj_rope("q", wqo, wqb, A_qro, qrob)
            wvo, wvb = load_piece(2)
            stageA(1)
            v64b = B("v64")

            def vtc(tc):
                for bk in range(2):
                    pt = pE if bk == 0 else pF
                    pre_read(PE, [lnxTb, wvb])
                    pre_write(PE, [pb(pt)])
                    for k in range(8):
                        lt = arn[:, A_lnxT + k * NCTX + 128 * tc:A_lnxT + k * NCTX + 128 * tc + 128]
                        rh = ring[:, wvo + k * 1024 + 512 * bk:wvo + k * 1024 + 512 * bk + 512]
                        i = mm(pt[:], lt, rh, k == 0, k == 7)
                    post(PE, i, writes=[pb(pt)], reads=[lnxTb, wvb])
                    if bk == 0:
                        pre_read(A, [pb(pt)])
                        pre_write(A, [vb])
                        i = nc.scalar.activation(
                            arn[:, A_v + 1024 * tc:A_v + 1024 * tc + 512],
                            pt[:], AF.Copy)
                        post(A, i, writes=[vb], reads=[pb(pt)])
                    else:
                        pre_read(V, [pb(pt)])
                        pre_write(V, [vb])
                        i = nc.vector.tensor_copy(
                            arn[:, A_v + 1024 * tc + 512:A_v + 1024 * tc + 1024],
                            pt[:])
                        post(V, i, writes=[vb], reads=[pb(pt)])
                if tc >= 1:
                    j = tc - 1
                    if j == 0:
                        qX.dma(arn[0:64, A_v64 + 1024 * j:A_v64 + 1024 * j + 1024],
                               arn[64:128, A_v + 1024 * j:A_v + 1024 * j + 1024],
                               writes=[v64b], reads=[vb])
                    else:
                        qX.dma_add(arn[0:64, A_v64 + 1024 * j:A_v64 + 1024 * j + 1024],
                                   arn[64:128, A_v + 1024 * j:A_v + 1024 * j + 1024],
                                   writes=[v64b], reads=[vb])
                    qX.dma_add(arn[64:128, A_v64 + 1024 * j:A_v64 + 1024 * j + 1024],
                               arn[0:64, A_v + 1024 * tc:A_v + 1024 * tc + 1024],
                               writes=[v64b], reads=[vb])

            _reg(f"s{s}:qkv_k")
            proj_rope("k", wko, wkb, A_kro, krob,
                      fillers=[lambda t=t: vtc(t) for t in range(5)])
            tap("qro", arn[:, A_qro:A_qro + 4096], [qrob], 4096)
            tap("kro", arn[:, A_kro:A_kro + 5120], [krob], 5120)
            ckpt(3)
            woo, wob = load_piece(3)
            use_set("sigmoid")
            stageB_mms(1, 0)
            rw1h = [None] * 4
            rw1h[0] = load_piece(4)
            _reg(f"s{s}:qkv_v")
            stageB_act(1, 0)
            stageB_mms(1, 1)
            stageB_act(1, 1)

            ckpt(4)
            tap("vv", arn[:, A_v:A_v + 5120], [vb], 5120)

            tap("v64", arn[:, A_v64:A_v64 + 4096], [v64b], 4096)

            ln2Tb = B("ln2T")

            def epilogue_pre(ot):
                tc = ot - 1
                y1b = B(f"y1_{ot}")
                zb = B(f"z{ot}")
                pre_read(V, [B("zss")])
                pre_write(V, [B("mg"), B("zrs")])
                selfedge(V, nc.vector.tensor_scalar(mgA[:, 6:7], zss[:, tc:tc + 1],
                                                    1.0 / D, EPS_RMS, OP.mult, OP.add))
                i = rsqrt_magic(zrs[:, tc:tc + 1], mgA[:, 6:7], 1)
                post(V, i, writes=[B("mg"), B("zrs")], reads=[B("zss")])
                pre_read(V, [zb, B("zrs"), B("c_scale")])
                pre_write(V, [zb])
                i = nc.vector.scalar_tensor_tensor(
                    z_sb[:, tc * D:ot * D], z_sb[:, tc * D:ot * D],
                    zrs[:, tc:tc + 1], c_scale[:], OP.mult, OP.mult)
                post(V, i, writes=[zb], reads=[B("zrs"), B("c_scale")])
                pre_read(G, [zb, xbufs[ot]])
                pre_write(G, [y1b])
                i = nc.gpsimd.tensor_tensor(
                    y1[:, tc * D:ot * D], z_sb[:, tc * D:ot * D],
                    x_sb[:, ot * D:(ot + 1) * D], OP.add)
                post(G, i, writes=[y1b], reads=[zb, xbufs[ot]])
                pre_read(V, [y1b])
                pre_write(V, [B("st6"), B("st2"), B("stat"), B("mg")])
                nc.vector.bn_stats(st6[:, 0:6], y1[:, tc * D:tc * D + 512])
                selfedge(V, nc.vector.bn_stats(st6[:, 6:12],
                                               y1[:, tc * D + 512:tc * D + 1024]))
                selfedge(V, nc.vector.bn_aggr(st2[:], st6[:]))
                selfedge(V, nc.vector.tensor_scalar(mgA[:, 7:8], st2[:, 1:2],
                                                    1.0, EPS_LN, OP.mult, OP.add))
                selfedge(V, rsqrt_magic(stat[:, 4 * tc + 1:4 * tc + 2], mgA[:, 7:8], 1))
                i = nc.vector.tensor_scalar(stat[:, 4 * tc + 2:4 * tc + 3], st2[:, 0:1],
                                            stat[:, 4 * tc + 1:4 * tc + 2], -1.0,
                                            OP.mult, OP.mult)
                post(V, i, writes=[B("st6"), B("st2"), B("stat"), B("mg")], reads=[y1b])
                xlb = B(f"xln{tc % 2}")
                xo = (tc % 2) * D
                pre_read(V, [y1b, B("stat")])
                pre_write(V, [xlb])
                i = nc.vector.tensor_scalar(xln[:, xo:xo + D], y1[:, tc * D:(tc + 1) * D],
                                            stat[:, 4 * tc + 1:4 * tc + 2],
                                            stat[:, 4 * tc + 2:4 * tc + 3],
                                            OP.mult, OP.add)
                post(V, i, writes=[xlb], reads=[y1b, B("stat")])

            def epilogue_post(ot):
                tc = ot - 1
                xlb = B(f"xln{tc % 2}")
                xo = (tc % 2) * D
                pre_read(PE, [xlb, B("c_eye")])
                pre_write(PE, [pb(pTR)])
                for ch in range(8):
                    i = nc.tensor.transpose(pTR[:, 128 * ch:128 * ch + 128],
                                            xln[:, xo + 128 * ch:xo + 128 * ch + 128],
                                            c_eye[:])
                post(PE, i, writes=[pb(pTR)], reads=[xlb, B("c_eye")])
                hb = B("ln2Th0") if tc <= 1 else B("ln2Th1")
                pre_read(V, [pb(pTR)])
                pre_write(V, [ln2Tb, hb])
                i = nc.vector.tensor_copy(
                    arn[:, F_ln2T:F_ln2T + 4096].rearrange(
                        "p (k c) -> p k c", k=8)[:, :, 128 * tc:128 * tc + 128],
                    pTR[:].rearrange("p (k c) -> p k c", k=8))
                post(V, i, writes=[ln2Tb, hb], reads=[pb(pTR)])

            def routing_mm(tc):
                pre_read(PE, [ln2Tb, B("c_rt"), B("c_or"), B("c_eb")])
                pre_write(PE, [pb(pM)])
                for k in range(8):
                    lt = arn[:, F_ln2T + k * 512 + 128 * tc:F_ln2T + k * 512 + 128 * tc + 128]
                    mm(pM[:, 384:392], lt, c_rt[:, 8 * k:8 * k + 8], k == 0, False)
                i = mm(pM[:, 384:392], c_or[:], c_eb[:], False, True)
                post(PE, i, writes=[pb(pM)],
                     reads=[ln2Tb, B("c_rt"), B("c_or"), B("c_eb")])
                pre_read(A, [pb(pM)])
                pre_write(A, [B("logi")])
                i = nc.scalar.activation(logi[:, 8 * tc:8 * tc + 8], pM[:, 384:392],
                                         AF.Copy)
                post(A, i, writes=[B("logi")], reads=[pb(pM)])

            def top2_pair(tcs):
                use_set("nlexp")
                n = 8 * len(tcs)
                o = 8 * tcs[0]
                pre_read(A, [B("logi")])
                pre_write(A, [B("rsc1"), B("rsc2")])
                selfedge(A, nc.scalar.activation(rsc1[:, o:o + n],
                                                 logi[:, o:o + n], AF.Abs))
                selfedge(A, nc.scalar.activation(rsc1[:, o:o + n], rsc1[:, o:o + n],
                                                 AF.Exp, scale=c_im[:, 5:6]))
                selfedge(A, nc.scalar.activation(rsc1[:, o:o + n], rsc1[:, o:o + n],
                                                 AF.Ln, bias=c_im[:, 1:2]))
                i = nc.scalar.activation(rsc2[:, o:o + n], logi[:, o:o + n], AF.Relu)
                post(A, i, writes=[B("rsc1"), B("rsc2")], reads=[B("logi")])
                pre_read(V, [B("rsc1"), B("rsc2")])
                pre_write(V, [B("rsc1")])
                i = nc.vector.tensor_tensor(rsc1[:, o:o + n], rsc1[:, o:o + n],
                                            rsc2[:, o:o + n], OP.add)
                post(V, i, writes=[B("rsc1")], reads=[B("rsc2")])
                pre_read(A, [B("rsc1")])
                pre_write(A, [B("rsc2")])
                selfedge(A, nc.scalar.activation(rsc2[:, o:o + n], rsc1[:, o:o + n],
                                                 AF.Ln))
                i = nc.scalar.activation(rsc2[:, o:o + n], rsc2[:, o:o + n],
                                         AF.Exp, scale=c_im[:, 6:7])
                post(A, i, writes=[B("rsc2")], reads=[B("rsc1")])
                for tc in tcs:
                    c = 8 * tc
                    pre_read(V, [B("rsc2")])
                    pre_write(V, [B("rsc3"), B("rsc1"), B("cf")])
                    selfedge(V, nc.vector.reduce_max(rsc3[:, 0:1], rsc2[:, c:c + 8],
                                                     mybir.AxisListType.X))
                    selfedge(V, nc.vector.tensor_scalar(
                        rsc1[:, c:c + 8], rsc2[:, c:c + 8], rsc3[:, 0:1],
                        -1e9, OP.is_equal, OP.mult))
                    selfedge(V, nc.vector.tensor_tensor(
                        rsc1[:, c:c + 8], rsc1[:, c:c + 8], rsc2[:, c:c + 8], OP.add))
                    selfedge(V, nc.vector.reduce_max(rsc3[:, 1:2], rsc1[:, c:c + 8],
                                                     mybir.AxisListType.X))
                    selfedge(V, nc.vector.tensor_scalar(
                        rsc1[:, c:c + 8], rsc2[:, c:c + 8], rsc3[:, 1:2],
                        0.0, OP.is_ge, OP.add))
                    i = nc.vector.tensor_tensor(cf_sb[:, 9 * tc:9 * tc + 8],
                                                rsc2[:, c:c + 8], rsc1[:, c:c + 8],
                                                OP.mult)
                    post(V, i, writes=[B("rsc3"), B("rsc1"), B("cf")],
                         reads=[B("rsc2")])

            def w1_half(src, src_off, srcb, dst_off, dstb, bias_ap, biasb, h,
                        banks, ltok):
                for m in range(4):
                    pt = banks[m]
                    pre_read(PE, [srcb, ltok, biasb])
                    pre_write(PE, [pb(pt)])
                    for k in range(8):
                        lt = src[:, src_off + k * 512 + 128 * m:src_off + k * 512 + 128 * m + 128]
                        rh = arn[:, F_ln2T + k * 512 + 256 * h:F_ln2T + k * 512 + 256 * h + 256]
                        i = mm(pt[:, 0:256], lt, rh, k == 0, k == 7)
                    post(PE, i, writes=[pb(pt)], reads=[srcb, ltok, biasb])
                    pre_read(A, [pb(pt), biasb])
                    pre_write(A, [dstb])
                    i = nc.scalar.activation(
                        arn[:, dst_off + 512 * m + 256 * h:dst_off + 512 * m + 256 * h + 256],
                        pt[:, 0:256], AF.Silu, bias=bias_ap[:, m:m + 1])
                    post(A, i, writes=[dstb], reads=[pb(pt), biasb])



            def stageC(ot, blk):
                B0 = 8 * s + 2 * ot
                Bg = B0 + blk
                mbase = 256 + 16 * (ot % 2)
                mtkb = B("mtk")
                msgTb = B("msgT")
                taub = B(f"tau{blk}")
                tauo = A_tau + 1024 * blk
                pl, ph = (pC, pD) if blk == 0 else (pE, pF)
                pre_read(PE, [taub, vb, B("v64"), B("c_oc")])
                pre_write(PE, [pb(pl), pb(ph), pb(pM)])
                for p in range(8):
                    pt = pl if p < 4 else ph
                    oc = 128 * (p % 4)
                    mc = mbase + 8 * blk + p
                    lt = arn[:, tauo + 128 * p:tauo + 128 * p + 128]
                    if Bg % 2 == 1:
                        vt = (Bg - 1) // 2 - 4 * s
                        rh = arn[:, A_v + 1024 * vt + 128 * p:A_v + 1024 * vt + 128 * p + 128]
                    else:
                        j = (Bg - 8 * s - 2) // 2
                        rh = arn[:, A_v64 + 1024 * j + 128 * p:A_v64 + 1024 * j + 128 * p + 128]
                    i = mm(pt[:, oc:oc + 128], lt, rh, True, True)
                    i = mm(pM[:, mc:mc + 1], lt, c_oc[:], True, True)
                post(PE, i, writes=[pb(pl), pb(ph), pb(pM)],
                     reads=[taub, vb, B("v64"), B("c_oc")])
                mo = A_mtk + 1024 * blk
                pre_read(V, [pb(pM)])
                pre_write(V, [B("rmas"), B("rm2")])
                selfedge(V, nc.vector.tensor_scalar(
                    rmas[:, 8 * blk:8 * blk + 8], pM[:, mbase + 8 * blk:mbase + 8 * blk + 8],
                    1e-6, 0.0, OP.max, OP.add))
                i = nc.vector.reciprocal(rm2[:, 8 * blk:8 * blk + 8],
                                         rmas[:, 8 * blk:8 * blk + 8])
                post(V, i, writes=[B("rmas"), B("rm2")], reads=[pb(pM)])
                pre_read(A, [pb(pl), B("rm2")])
                pre_write(A, [mtkb])
                for p in range(4):
                    idx = 8 * blk + p
                    i = nc.scalar.activation(
                        arn[:, mo + 128 * p:mo + 128 * p + 128],
                        pl[:, 128 * p:128 * p + 128], AF.Copy,
                        scale=rm2[:, idx:idx + 1])
                post(A, i, writes=[mtkb], reads=[pb(pl), B("rm2")])
                pre_read(V, [pb(ph), B("rm2")])
                pre_write(V, [mtkb])
                for p in range(4, 8):
                    idx = 8 * blk + p
                    i = nc.vector.tensor_scalar(
                        arn[:, mo + 128 * p:mo + 128 * p + 128],
                        ph[:, 128 * (p % 4):128 * (p % 4) + 128],
                        rm2[:, idx:idx + 1], 0.0, OP.mult, OP.add)
                post(V, i, writes=[mtkb], reads=[pb(ph), B("rm2")])
                pre_read(PE, [mtkb, B("c_eye")])
                pre_write(PE, [pb(pl), pb(ph)])
                for p in range(8):
                    pt = pl if p < 4 else ph
                    oc = 128 * (p % 4)
                    i = mm(pt[:, oc:oc + 128],
                           arn[:, mo + 128 * p:mo + 128 * p + 128],
                           c_eye[:], True, True)
                post(PE, i, writes=[pb(pl), pb(ph)], reads=[mtkb, B("c_eye")])
                pre_read(A, [pb(pl), pb(ph)])
                pre_write(A, [msgTb])
                for par, (q0, q1) in enumerate(((0, 64), (64, 128))):
                    nc.scalar.copy(
                        arn[q0:q1, A_msgT:A_msgT + 512].rearrange(
                            "p (g c) -> p g c", g=4)[:, :, 64 * blk:64 * blk + 64],
                        pl[q0:q1, :].rearrange(
                            "p (g c) -> p g c", g=4)[:, :, q0:q0 + 64])
                    i = nc.scalar.copy(
                        arn[q0:q1, A_msgT + 512:A_msgT + 1024].rearrange(
                            "p (g c) -> p g c", g=4)[:, :, 64 * blk:64 * blk + 64],
                        ph[q0:q1, :].rearrange(
                            "p (g c) -> p g c", g=4)[:, :, q0:q0 + 64])
                post(A, i, writes=[msgTb], reads=[pb(pl), pb(ph)])

            def stageW(ot):
                msgTb = B("msgT")
                pre_read(PE, [msgTb, wob])
                pre_write(PE, [pb(pA), pb(pB)])
                for bk in range(2):
                    pt = pA if bk == 0 else pB
                    for ch in range(8):
                        lt = arn[:, A_msgT + 128 * ch:A_msgT + 128 * ch + 128]
                        rh = ring[:, woo + ch * 1024 + 512 * bk:woo + ch * 1024 + 512 * bk + 512]
                        i = mm(pt[:], lt, rh, ch == 0, ch == 7)
                post(PE, i, writes=[pb(pA), pb(pB)], reads=[msgTb, wob])

            def stageZ1(ot):
                ct = ot
                zb = B(f"z{ot}")
                pre_read(A, [pb(pA), pb(pB)])
                pre_write(A, [zb])
                nc.scalar.copy(z_sb[:, (ot - 1) * D:(ot - 1) * D + 512], pA[:])
                i = nc.scalar.copy(z_sb[:, (ot - 1) * D + 512:ot * D], pB[:])
                post(A, i, writes=[zb], reads=[pb(pA), pb(pB)])
                pre_read(V, [zb, xbufs[ct], B("stat")])
                pre_write(V, [zb])
                selfedge(V, nc.vector.scalar_tensor_tensor(
                    z_sb[:, (ot - 1) * D:ot * D],
                    x_sb[:, ct * D:(ct + 1) * D],
                    stat[:, 4 * ct + 1:4 * ct + 2],
                    z_sb[:, (ot - 1) * D:ot * D], OP.mult, OP.add))
                i = nc.vector.tensor_scalar(
                    z_sb[:, (ot - 1) * D:ot * D], z_sb[:, (ot - 1) * D:ot * D],
                    stat[:, 4 * ct + 2:4 * ct + 3], 0.0, OP.add, OP.add)
                post(V, i, writes=[zb], reads=[xbufs[ct], B("stat")])

            def stageZ2(ot):
                zb = B(f"z{ot}")
                pre_read(A, [zb])
                pre_write(A, [B("zss"), B("trash")])
                i = nc.scalar.activation(trash[:], z_sb[:, (ot - 1) * D:ot * D],
                                         AF.Square, accum_out=zss[:, ot - 1:ot])
                post(A, i, writes=[B("zss"), B("trash")], reads=[zb])

            _reg(f"s{s}:attn")
            for ot in range(1, 5):
                _reg(f"s{s}:at{ot}")
                stageC(ot, 0)
                if ot >= 2:
                    stageZ2(ot - 1)
                if ot >= 3:
                    epilogue_post(ot - 2)
                stageC(ot, 1)
                if ot < 4:
                    stageA(ot + 1)
                    stageB(ot + 1)
                else:
                    use_set("silu")
                    routing_mm(0)
                    routing_mm(1)
                stageW(ot)
                if ot >= 2:
                    epilogue_pre(ot - 1)
                if ot == 4:
                    rw1h[1] = load_piece(5)
                    w1_half(c_sw1, 0, B("c_sw1"), F_sh, B("sh"), c_sb1,
                            B("c_sb1"), 0, (pC, pD, pE, pF), B("ln2Th0"))
                    w1_half(ring, rw1h[0][0], rw1h[0][1], F_g[0], B("g0"),
                            c_rb1[:, 0:4], B("c_rb1"), 0, (pC, pD, pE, pF),
                            B("ln2Th0"))
                    w1_half(ring, rw1h[0][0] + 4096, rw1h[0][1], F_g[1], B("g1"),
                            c_rb1[:, 4:8], B("c_rb1"), 0, (pC, pD, pE, pF),
                            B("ln2Th0"))
                stageZ1(ot)
            _reg(f"s{s}:epi")
            stageZ2(4)
            epilogue_post(3)
            routing_mm(2)
            epilogue_pre(4)
            epilogue_post(4)
            routing_mm(3)
            w1_half(ring, rw1h[1][0], rw1h[1][1], F_g[2], B("g2"),
                    c_rb1[:, 8:12], B("c_rb1"), 0, (pC, pD, pE, pF),
                    B("ln2Th0"))
            w1_half(ring, rw1h[1][0] + 4096, rw1h[1][1], F_g[3], B("g3"),
                    c_rb1[:, 12:16], B("c_rb1"), 0, (pC, pD, pE, pF),
                    B("ln2Th0"))
            top2_pair((0, 1, 2, 3))
            tap("z", z_sb[:], [B("z1"), B("z2"), B("z3"), B("z4")], 4096)
            ckpt(5)
            tap("y1", y1[:], [B("y1_1"), B("y1_2"), B("y1_3"), B("y1_4")], 4096)
            tap("ln2T", arn[:, F_ln2T:F_ln2T + 4096], [ln2Tb], 4096)
            tap("cf", cf_sb[:], [B("cf")], 36)
            ckpt(6)
            _reg(f"s{s}:w1sh")
            use_set("silu")

            def w1_block(src, src_off, srcb, dst_off, dstb, bias_ap, biasb):
                for m in range(4):
                    pt = (pA, pB, pD, pE)[m]
                    pre_read(PE, [srcb, ln2Tb])
                    pre_write(PE, [pb(pt)])
                    for k in range(8):
                        lt = src[:, src_off + k * 512 + 128 * m:src_off + k * 512 + 128 * m + 128]
                        rh = arn[:, F_ln2T + k * 512:F_ln2T + k * 512 + 512]
                        i = mm(pt[:], lt, rh, k == 0, k == 7)
                    post(PE, i, writes=[pb(pt)], reads=[srcb, ln2Tb])
                    pre_read(A, [pb(pt), biasb])
                    pre_write(A, [dstb])
                    i = nc.scalar.activation(
                        arn[:, dst_off + 512 * m:dst_off + 512 * m + 512], pt[:],
                        AF.Silu, bias=bias_ap[:, m:m + 1])
                    post(A, i, writes=[dstb], reads=[pb(pt), biasb])
            shb = B("sh")
            w1_half(c_sw1, 0, B("c_sw1"), F_sh, shb, c_sb1, B("c_sb1"), 1,
                    (pA, pB, pD, pE), ln2Tb)
            use_set("silu")
            gb = [B(f"g{e}") for e in range(8)]
            for e in range(8):
                _reg(f"s{s}:w1e{e}")
                pr = e // 2
                if e % 2 == 0 and pr > 0:
                    pass
                eo = (e % 2) * 4096
                rwo, rwb = rw1h[pr]
                if e <= 3:
                    w1_half(ring, rwo + eo, rwb, F_g[e], gb[e],
                            c_rb1[:, 4 * e:4 * e + 4], B("c_rb1"), 1,
                            (pA, pB, pD, pE), ln2Tb)
                else:
                    w1_block(ring, rwo + eo, rwb, F_g[e], gb[e],
                             c_rb1[:, 4 * e:4 * e + 4], B("c_rb1"))
                dcfb = B("dcf")
                pre_read(V, [B("cf"), B("c_eye")])
                pre_write(V, [dcfb])
                for tc in range(4):
                    i = nc.vector.tensor_scalar(
                        arn[:, F_dcf + 128 * tc:F_dcf + 128 * tc + 128], c_eye[:],
                        cf_sb[:, 9 * tc + e:9 * tc + e + 1], 0.0, OP.mult, OP.add)
                post(V, i, writes=[dcfb], reads=[B("cf"), B("c_eye")])
                cbcb = B("cbc")
                pre_read(PE, [dcfb, B("c_ones")])
                pre_write(PE, [pb(pC)])
                for tc in range(4):
                    i = mm(pC[:, 128 * tc:128 * tc + 128], c_ones[:],
                           arn[:, F_dcf + 128 * tc:F_dcf + 128 * tc + 128], True, True)
                post(PE, i, writes=[pb(pC)], reads=[dcfb, B("c_ones")])
                pre_read(V, [pb(pC)])
                pre_write(V, [B("cbc")])
                i = nc.vector.tensor_copy(arn[:, F_cbc:F_cbc + 512], pC[:])
                post(V, i, writes=[B("cbc")], reads=[pb(pC)])
                for m in range(4):
                    pre_read(G, [gb[e], B("cbc")])
                    pre_write(G, [gb[e]])
                    i = nc.gpsimd.tensor_tensor(
                        arn[:, F_g[e] + 512 * m:F_g[e] + 512 * m + 512],
                        arn[:, F_g[e] + 512 * m:F_g[e] + 512 * m + 512],
                        arn[:, F_cbc:F_cbc + 512], OP.mult)
                    post(G, i, writes=[gb[e]], reads=[B("cbc")])
                if e in (3, 5):
                    rw1h[e // 2 + 1] = load_piece(4 + e // 2 + 1)

            tap("sh", arn[:, F_sh:F_sh + 2048], [shb], 2048)
            tap("g0", arn[:, F_g[0]:F_g[0] + 2048], [B("g0")], 2048)
            tap("g7", arn[:, F_g[7]:F_g[7] + 2048], [B("g7")], 2048)
            ckpt(7)
            pre_read(V, [B("cf")])
            pre_write(V, [B("cfb")])
            i = nc.vector.tensor_copy(cfb[:], cf_sb[:])
            post(V, i, writes=[B("cfb")], reads=[B("cf")])
            pre_read(PE, [B("cfb"), B("c_eye")])
            pre_write(PE, [pb(pTR)])
            for tc in range(4):
                i = nc.tensor.transpose(pTR[0:9, 128 * tc:128 * tc + 128],
                                        cfb[:, 9 * tc:9 * tc + 9], c_eye[:])
            post(PE, i, writes=[pb(pTR)], reads=[B("cfb"), B("c_eye")])
            pre_read(V, [pb(pTR)])
            pre_write(V, [B("cfT")])
            i = nc.vector.tensor_copy(cfT[:], pTR[0:9, 0:512])
            post(V, i, writes=[B("cfT")], reads=[pb(pTR)])
            for grp in range(2):
                _reg(f"s{s}:w2g{grp}")
                tcs = (0, 1) if grp == 0 else (2, 3)
                pos = {tcs[0]: (pD, pE), tcs[1]: (pF, pC)}
                started = {}
                for tc in tcs:
                    pl, ph = pos[tc]
                    pre_read(PE, [shb, B("cfT"), B("c_rb2"), B("c_sw2")])
                    pre_write(PE, [pb(pl), pb(ph)])
                    for bk in range(2):
                        pt = pl if bk == 0 else ph
                        for m in range(4):
                            lt = arn[:, F_sh + m * 512 + 128 * tc:F_sh + m * 512 + 128 * tc + 128]
                            rh = c_sw2[:, m * 1024 + 512 * bk:m * 1024 + 512 * bk + 512]
                            i = mm(pt[:], lt, rh, m == 0, False)
                        i = mm(pt[:], cfT[:, 128 * tc:128 * tc + 128],
                               c_rb2[:, 512 * bk:512 * bk + 512], False, False)
                    post(PE, i, writes=[pb(pl), pb(ph)],
                         reads=[shb, B("cfT"), B("c_rb2"), B("c_sw2")])
                for pr in range(4):
                    rw2o, rw2b = load_piece(8 + pr)
                    for tc in tcs:
                        pl, ph = pos[tc]
                        pre_read(PE, [rw2b] + [gb[2 * pr], gb[2 * pr + 1]])
                        pre_write(PE, [pb(pl), pb(ph)])
                        for ei in range(2):
                            e = 2 * pr + ei
                            eo = 4096 * ei
                            for bk in range(2):
                                pt = pl if bk == 0 else ph
                                for m in range(4):
                                    lt = arn[:, F_g[e] + m * 512 + 128 * tc:
                                             F_g[e] + m * 512 + 128 * tc + 128]
                                    rh = ring[:, rw2o + eo + m * 1024 + 512 * bk:
                                              rw2o + eo + m * 1024 + 512 * bk + 512]
                                    last = pr == 3 and ei == 1 and m == 3
                                    i = mm(pt[:], lt, rh, False, last)
                        post(PE, i, writes=[pb(pl), pb(ph)],
                             reads=[rw2b, gb[2 * pr], gb[2 * pr + 1]])
                if grp == 1 and s + 1 < NSTI:
                    _reg(f"s{s}:prolog")
                    STV[s + 1] = prologue(s + 1)
                    _reg(f"s{s}:w2tail")
                _reg(f"s{s}:y1o{grp}")
                for tc in tcs:
                    pl, ph = pos[tc]
                    y1b = B(f"y1_{tc + 1}")
                    pre_read(V, [pb(pl), pb(ph)])
                    pre_write(V, [y1b])
                    nc.vector.tensor_tensor(y1[:, tc * D:tc * D + 512],
                                            y1[:, tc * D:tc * D + 512], pl[:], OP.add)
                    i = nc.vector.tensor_tensor(y1[:, tc * D + 512:(tc + 1) * D],
                                                y1[:, tc * D + 512:(tc + 1) * D],
                                                ph[:], OP.add)
                    post(V, i, writes=[y1b], reads=[pb(pl), pb(ph)])
                    orow = r0 + 128 * tc
                    qO.dma(OUT[orow:orow + 128, :], y1[:, tc * D:(tc + 1) * D],
                           reads=[y1b])


    except _Stop:
        pass
    for q in (qW, qX, qO):
        for ln in q.lanes:
            q.e.wait_ge(ln.h, ln.n)
    for ln in qO.lanes:
        nc.sync.wait_ge(ln.h, ln.n)
        nc.vector.wait_ge(ln.h, ln.n)
    stack.close()
    return nc


def host_in_maps(L0_attn_x, L0_attn_wq, L0_attn_wk, L0_attn_wv, L0_attn_wo,
                 L0_attn_norm_scale, L0_ffn_routing_W, L0_ffn_expert_bias,
                 L0_ffn_sW1, L0_ffn_sb1, L0_ffn_sW2, L0_ffn_sb2,
                 L0_ffn_rW1, L0_ffn_rb1, L0_ffn_rW2, L0_ffn_rb2, ncores=8):
    x = np.asarray(L0_attn_x, dtype=np.float32)
    nsc = np.asarray(L0_attn_norm_scale, np.float32)
    rb1 = np.asarray(L0_ffn_rb1, np.float32)
    sb1 = np.asarray(L0_ffn_sb1, np.float32)
    rb1c = np.empty((128, 32), np.float32)
    for e in range(8):
        for k in range(4):
            rb1c[:, 4 * e + k] = rb1[e, 128 * k:128 * k + 128]
    sb1c = np.empty((128, 4), np.float32)
    for k in range(4):
        sb1c[:, k] = sb1[128 * k:128 * k + 128]
    def tile_w(w, kk):
        w = np.asarray(w, np.float32)
        return np.ascontiguousarray(
            w.reshape(kk, 128, -1).transpose(1, 0, 2).reshape(128, -1))

    shared = dict(
        wq=tile_w(L0_attn_wq, 8),
        wk=tile_w(L0_attn_wk, 8),
        wv=tile_w(L0_attn_wv, 8),
        wo=tile_w(L0_attn_wo, 8),
        sw1=tile_w(L0_ffn_sW1, 8),
        sw2=tile_w(L0_ffn_sW2, 4),
        rw1=np.stack([tile_w(np.asarray(L0_ffn_rW1, np.float32)[e], 8)
                      for e in range(8)]),
        rw2=np.stack([tile_w(np.asarray(L0_ffn_rW2, np.float32)[e], 4)
                      for e in range(8)]),
        routingT=tile_w(np.asarray(L0_ffn_routing_W, np.float32).T, 8),
        rb2sb2=np.ascontiguousarray(
            np.concatenate([np.asarray(L0_ffn_rb2, np.float32),
                            np.asarray(L0_ffn_sb2, np.float32)[None, :]], 0)),
        ebias=np.ascontiguousarray(np.asarray(L0_ffn_expert_bias, np.float32)[None, :]),
        rb1c=rb1c, sb1c=sb1c,
        scale_bc=np.ascontiguousarray(np.tile(nsc[None, :], (128, 1))),
    )
    in_maps = []
    for core in range(ncores):
        b, p0 = core // 4, (core % 4) * TPC
        xc = np.zeros((XR, D), np.float32)
        if p0 > 0:
            xc[64:128] = x[b, p0 - 64:p0]
        xc[128:] = x[b, p0:p0 + TPC]
        m = dict(shared)
        m["x"] = xc
        m.update(_host_consts(p0))
        in_maps.append(m)
    return in_maps


def kernel(**inputs):
    in_maps = host_in_maps(**inputs)
    nc = build_program()
    res = run_bass_kernel_spmd(nc, in_maps, core_ids=list(range(8)))
    out = np.empty((2, 16384, D), np.float32)
    for core in range(8):
        b, p0 = core // 4, (core % 4) * TPC
        out[b, p0:p0 + TPC] = res.results[core]["y"]
    return out


# version 23
# speedup vs baseline: 1.0020x; 1.0020x over previous
import contextlib
import numpy as np
import ml_dtypes
import concourse.bass as bass
import concourse.mybir as mybir
from concourse.bass_utils import run_bass_kernel_spmd
from concourse.alu_op_type import AluOpType as OP

F32 = mybir.dt.float32
BF16 = mybir.dt.bfloat16
AF = mybir.ActivationFunctionType

D = 1024
TPC = 4096
XR = TPC + 128
NSTI = 8
NCTX = 640
EPS_LN = 1e-5
EPS_RMS = 1e-6
ROPE_BASE = 10000.0
PIECE = 8192
PHASE = 99
NOMASS = False
NOEVEN = False
DUMP = None
PROF_REGIONS = False
REGIONS = []


class _Stop(Exception):
    pass

_BP = [("wq", 0, 8192), ("wk", 8192, 8192), ("wv", 16384, 8192),
       ("wo", 24576, 8192)]
for _i in range(4):
    _BP.append((f"rw1_{_i}", 36864 + 8192 * _i, 8192))
for _i in range(4):
    _BP.append((f"rw2_{_i}", 69632 + 8192 * _i, 8192))
BLOB_COLS = 69632 + 8192 * 4


def _host_consts(p0):
    half = 32
    theta = 1.0 / ROPE_BASE ** (np.arange(half, dtype=np.float64) / half)
    pos = np.clip(p0 + np.arange(XR) - 128, 0, None).astype(np.float64)
    fr = pos[None, :] * theta[:, None]
    cct = np.tile(np.cos(fr), (4, 1)).astype(ml_dtypes.bfloat16)
    sst = np.tile(np.sin(fr), (4, 1)).astype(ml_dtypes.bfloat16)

    c = np.arange(128)[:, None]
    j = np.arange(64)[None, :]
    band = ((c >= j) & (c <= j + 63)).astype(np.float32)
    m_main = np.concatenate([band, band], axis=1)
    bf = band * (c >= 64) if p0 == 0 else band
    m_first = np.concatenate([bf, bf], axis=1)

    P = np.zeros((128, 128), np.float32)
    for m in range(128):
        g, r = m // 64 * 64, m % 64
        if r < 32:
            P[g + r + 32, m] = -1.0
        else:
            P[g + r - 32, m] = 1.0
    return dict(
        cct=cct, sstb=sst,
        mask_main=np.tile(m_main, (1, 8)).astype(ml_dtypes.bfloat16),
        mask_first=np.tile(m_first, (1, 8)).astype(ml_dtypes.bfloat16),
        prot=P.astype(ml_dtypes.bfloat16),
        eye=np.eye(128, dtype=ml_dtypes.bfloat16),
        onesmat=np.ones((128, 128), dtype=ml_dtypes.bfloat16),
        onesrow=np.ones((1, 128), dtype=ml_dtypes.bfloat16),
        onescol=np.ones((128, 1), dtype=ml_dtypes.bfloat16),
    )


class Sem:
    def __init__(self, stack, nc, name):
        self.h = stack.enter_context(nc.semaphore(name))
        self.n = 0


class Eng:
    def __init__(self, e, sem, step=1):
        self.e = e
        self.sem = sem
        self.step = step
        self.waited = {}

    def wait(self, tok):
        if tok is None:
            return
        sem, v = tok
        if self.waited.get(id(sem), -1) >= v:
            return
        self.e.wait_ge(sem.h, v)
        self.waited[id(sem)] = v

    def fence(self, instr):
        instr.then_inc(self.sem.h, self.step)
        self.sem.n += self.step
        return (self.sem, self.sem.n)


class Buf:
    __slots__ = ("w", "rs", "name", "aliases")

    def __init__(self, name=""):
        self.w = {}
        self.rs = {}
        self.name = name
        self.aliases = []

    @staticmethod
    def _add(d, tok):
        cur = d.get(id(tok[0]))
        if cur is None or tok[1] > cur[1]:
            d[id(tok[0])] = tok

    def add_r(self, tok):
        self._add(self.rs, tok)

    def add_w(self, tok):
        self._add(self.w, tok)


def pre_write(E, bufs):
    for b in bufs:
        for t in b.w.values():
            E.wait(t)
        for t in b.rs.values():
            E.wait(t)
        for a in b.aliases:
            for t in a.w.values():
                E.wait(t)
            for t in a.rs.values():
                E.wait(t)


def pre_read(E, bufs):
    for b in bufs:
        for t in b.w.values():
            E.wait(t)


def selfedge(E, instr):
    E.wait(E.fence(instr))


def post(E, instr, writes=(), reads=()):
    tok = E.fence(instr)
    for b in writes:
        b.w = {id(tok[0]): tok}
        b.rs = {}
    for b in reads:
        b.add_r(tok)
    return tok


class DQueue:

    def __init__(self, e, lanes):
        self.e = e
        self.lanes = lanes
        self.i = 0
        self.waited = {}

    def wait(self, tok):
        if tok is None:
            return
        sem, v = tok
        if self.waited.get(id(sem), -1) >= v:
            return
        self.e.wait_ge(sem.h, v)
        self.waited[id(sem)] = v

    def dma(self, out, in_, writes=(), reads=()):
        ln = self.lanes[self.i % len(self.lanes)]
        self.i += 1
        self.wait((ln, ln.n))
        for b in writes:
            pre_write(self, [b])
        for b in reads:
            pre_read(self, [b])
        instr = self.e.dma_start(out=out, in_=in_)
        instr.then_inc(ln.h, 16)
        ln.n += 16
        tok = (ln, ln.n)
        for b in writes:
            b.w = {id(ln): tok}
            b.rs = {}
        for b in reads:
            b.add_r(tok)
        return tok

    def dma_add(self, out, in_, writes=(), reads=()):
        ln = self.lanes[self.i % len(self.lanes)]
        self.i += 1
        self.wait((ln, ln.n))
        for b in writes:
            pre_write(self, [b])
        for b in reads:
            pre_read(self, [b])
        instr = self.e.dma_start(out=out, in_=in_)
        instr.then_inc(ln.h, 16)
        ln.n += 16
        tok = (ln, ln.n)
        for b in writes:
            b.add_w(tok)
        for b in reads:
            b.add_r(tok)
        return tok


def build_program():
    stack = contextlib.ExitStack()
    nc = bass.Bass("TRN2", target_bir_lowering=False)

    def din(name, shape, dt):
        return nc.dram_tensor(name, list(shape), dt, kind="ExternalInput").ap()

    X = din("x", [XR, D], F32)
    Wq = din("wq", [128, 8192], F32); Wk = din("wk", [128, 8192], F32)
    Wv = din("wv", [128, 8192], F32); Wo = din("wo", [128, 8192], F32)
    SW1 = din("sw1", [128, 4096], F32); SW2 = din("sw2", [128, 4096], F32)
    RW1 = din("rw1", [8, 128, 4096], F32); RW2 = din("rw2", [8, 128, 4096], F32)
    RoutT = din("routingT", [128, 64], F32)
    Rb2Sb2 = din("rb2sb2", [9, D], F32)
    EBias = din("ebias", [1, 8], F32)
    Rb1c = din("rb1c", [128, 32], F32)
    Sb1c = din("sb1c", [128, 4], F32)
    ScaleBC = din("scale_bc", [128, D], F32)
    CCT = din("cct", [128, XR], BF16); SSTd = din("sstb", [128, XR], BF16)
    MaskM = din("mask_main", [128, 1024], BF16)
    MaskF = din("mask_first", [128, 1024], BF16)
    ProtD = din("prot", [128, 128], BF16)
    EyeD = din("eye", [128, 128], BF16)
    OnesMD = din("onesmat", [128, 128], BF16)
    OnesRD = din("onesrow", [1, 128], BF16)
    OnesCD = din("onescol", [128, 1], BF16)
    OUT = nc.dram_tensor("y", [TPC, D], F32, kind="ExternalOutput").ap()
    DBG = nc.dram_tensor("dbg", [128, 8192], F32, kind="ExternalOutput").ap() \
        if DUMP is not None else None
    BLOB = nc.dram_tensor("wblob", [128, BLOB_COLS], BF16, kind="Internal").ap()

    def sbt(name, shape, dt):
        return stack.enter_context(nc.sbuf_tensor(name, list(shape), dt))

    ring = sbt("ring", [128, 2 * PIECE], BF16)
    c_sw2 = sbt("c_sw2", [128, 4096], BF16)
    c_sw1 = sbt("c_sw1", [128, 4096], BF16)
    cc_s = sbt("cc_s", [128, 2 * NCTX], BF16)
    ss_s = sbt("ss_s", [128, 2 * NCTX], BF16)
    x_sb = sbt("x_sb", [128, 5 * D], F32)
    xln = sbt("xln", [128, 2 * D], BF16)
    arn = sbt("arn", [128, 36608], BF16)
    z_sb = sbt("z_sb", [128, 4 * D], F32)
    y1 = sbt("y1", [128, 4 * D], F32)
    trash = sbt("trash", [128, D], F32)
    c_mm = sbt("c_mm", [128, 1024], BF16)
    c_mf = sbt("c_mf", [128, 1024], BF16)
    c_eye = sbt("c_eye", [128, 128], BF16)
    c_ones = sbt("c_ones", [128, 128], BF16)
    c_prot = sbt("c_prot", [128, 128], BF16)
    c_or = sbt("c_or", [1, 128], BF16)
    c_oc = sbt("c_oc", [128, 1], BF16)
    c_scale = sbt("c_scale", [128, D], F32)
    c_rb1 = sbt("c_rb1", [128, 32], F32)
    c_sb1 = sbt("c_sb1", [128, 4], F32)
    c_rt = sbt("c_rt", [128, 64], BF16)
    c_rb2 = sbt("c_rb2", [9, D], BF16)
    c_eb = sbt("c_eb", [1, 8], BF16)
    c_im = sbt("c_im", [128, 8], F32)
    mgA = sbt("mgA", [128, 8], F32)
    mgB = sbt("mgB", [128, 8], F32)
    mgC = sbt("mgC", [128, 8], F32)
    st6 = sbt("st6", [128, 12], F32)
    st2 = sbt("st2", [128, 2], F32)
    stat = sbt("stat", [128, 5 * 4], F32)
    zss = sbt("zss", [128, 4], F32)
    zrs = sbt("zrs", [128, 4], F32)
    rmas = sbt("rmas", [128, 16], F32)
    rm2 = sbt("rm2", [128, 16], F32)
    logi = sbt("logi", [128, 32], F32)
    rsc1 = sbt("rsc1", [128, 32], F32)
    rsc2 = sbt("rsc2", [128, 32], F32)
    rsc3 = sbt("rsc3", [128, 2], F32)
    cf_sb = sbt("cf_sb", [128, 9 * 4], F32)
    cfb = sbt("cfb", [128, 9 * 4], BF16)
    cfT = sbt("cfT", [9, 4 * 128], BF16)

    off = 0
    def _take(n):
        nonlocal off
        o = off; off += n; return o
    A_lnxT = _take(8 * NCTX)
    A_qro = _take(8 * 512)
    A_kro = _take(8 * NCTX)
    A_v = _take(5 * 1024)
    A_v64 = _take(4 * 1024)
    A_qd = _take(2 * 1024)
    A_tau = _take(2 * 1024)
    A_mtk = _take(2 * 1024)
    A_dg = _take(16 * 128)
    A_msgT = _take(8 * 128)
    A_qtp = _take(2 * NCTX)
    A_t1 = _take(2 * NCTX)
    A_t2 = _take(2 * NCTX)
    assert off <= 36608, off
    ATTN_REG = [(["lnxT"], A_lnxT, 5120), (["qro"], A_qro, 4096),
                (["kro"], A_kro, 5120), (["vv"], A_v, 5120),
                (["v64"], A_v64, 4096), (["qd"], A_qd, 2048),
                (["tau0", "tau1"], A_tau, 2048), (["mtk"], A_mtk, 2048),
                (["dg"], A_dg, 2048), (["msgT"], A_msgT, 1024),
                (["qtp0", "qtp1"], A_qtp, 1280), (["t1_0", "t1_1"], A_t1, 1280),
                (["t2_0", "t2_1"], A_t2, 1280)]
    off2 = 0
    def _take2(n):
        nonlocal off2
        o = off2; off2 += n; return o
    F_ln2T = _take2(8 * 512)
    F_sh = _take2(4 * 512)
    F_g = [_take2(4 * 512) for _ in range(8)]
    F_cbc = _take2(512)
    F_dcf = _take2(4 * 128)
    FFN_REG = [(["ln2T"], F_ln2T, 4096), (["sh"], F_sh, 2048)] + \
              [([f"g{e}"], F_g[e], 2048) for e in range(8)] + \
              [(["cbc"], F_cbc, 512), (["dcf"], F_dcf, 512)]
    assert off2 <= off

    bufs = {}
    def B(name):
        if name not in bufs:
            bufs[name] = Buf(name)
        return bufs[name]

    for ans, ao, al in ATTN_REG:
        for fns, fo, fl in FFN_REG:
            if ao < fo + fl and fo < ao + al:
                for an in ans:
                    for fn in fns:
                        B(an).aliases.append(B(fn))
                        B(fn).aliases.append(B(an))

    def ps(name, shape, dt):
        return stack.enter_context(nc.psum_tensor(name, list(shape), dt))
    pA = ps("pA", [128, 512], F32); pB = ps("pB", [128, 512], F32)
    pC = ps("pC", [128, 512], F32); pD = ps("pD", [128, 512], F32)
    pE = ps("pE", [128, 512], F32); pF = ps("pF", [128, 512], F32)
    pTR = ps("pTR", [128, 1024], BF16)
    pM = ps("pM", [128, 512], F32)
    PB = {id(pA): B("pA"), id(pB): B("pB"), id(pC): B("pC"), id(pD): B("pD"),
          id(pE): B("pE"), id(pF): B("pF"), id(pTR): B("pTR"), id(pM): B("pM")}
    def pb(t):
        return PB[id(t)]

    qW = DQueue(nc.sync, [Sem(stack, nc, f"qW{i}") for i in range(4)])
    qX = DQueue(nc.gpsimd, [Sem(stack, nc, f"qX{i}") for i in range(4)])
    qO = DQueue(nc.scalar, [Sem(stack, nc, f"qO{i}") for i in range(2)])
    PE = Eng(nc.tensor, Sem(stack, nc, "sP"))
    V = Eng(nc.vector, Sem(stack, nc, "sV"))
    A = Eng(nc.scalar, Sem(stack, nc, "sA"))
    G = Eng(nc.gpsimd, Sem(stack, nc, "sG"))

    for dst, src, nm in [
            (c_mm[:], MaskM, "c_mm"), (c_mf[:], MaskF, "c_mf"),
            (c_eye[:], EyeD, "c_eye"), (c_ones[:], OnesMD, "c_ones"),
            (c_prot[:], ProtD, "c_prot"), (c_or[:], OnesRD, "c_or"),
            (c_oc[:], OnesCD, "c_oc"), (c_scale[:], ScaleBC, "c_scale"),
            (c_rb1[:], Rb1c, "c_rb1"), (c_sb1[:], Sb1c, "c_sb1"),
            (c_rb2[:], Rb2Sb2, "c_rb2"), (c_eb[:], EBias, "c_eb")]:
        qX.dma(dst, src, writes=[B(nm)])
    qX.dma(c_rt[:], RoutT, writes=[B("c_rt")])
    qX.dma(c_sw2[:], SW2, writes=[B("c_sw2")])
    qX.dma(c_sw1[:], SW1, writes=[B("c_sw1")])

    i = nc.vector.memset(arn[:, A_qd:A_qd + 2048], 0.0)
    post(V, i, writes=[B("qd")])
    i = nc.vector.memset(cf_sb[:], 1.0)
    post(V, i, writes=[B("cf")])
    for _c, _v in enumerate([EPS_LN, 1.0, 1.0 / D, EPS_RMS, -0.5, -1.0, 0.5, 0.125]):
        i = nc.vector.memset(c_im[:, _c:_c + 1], _v)
    post(V, i, writes=[B("c_im")])
    A.wait((V.sem, V.sem.n))

    CAST = {"wq": [(0, Wq)], "wk": [(0, Wk)], "wv": [(0, Wv)], "wo": [(0, Wo)]}
    for _e in range(4):
        CAST[f"rw1_{_e}"] = [(0, RW1[2 * _e]), (4096, RW1[2 * _e + 1])]
        CAST[f"rw2_{_e}"] = [(0, RW2[2 * _e]), (4096, RW2[2 * _e + 1])]
    slotbuf = [B("slot0"), B("slot1")]
    blobbuf = [B(f"blob{pi}") for pi in range(len(_BP))]

    def emit_casts():
        prev = None
        for pi, (nm, boff, ncols) in enumerate(_BP):
            csem = Sem(stack, nc, f"cs{pi}")
            if pi <= 4 and prev is not None:
                nc.gpsimd.wait_ge(prev.h, prev.n)
            for (off_, srcap) in CAST[nm]:
                w = srcap.shape[-1]
                i = nc.gpsimd.dma_start(out=BLOB[:, boff + off_:boff + off_ + w],
                                        in_=srcap)
                i.then_inc(csem.h, 16)
                csem.n += 16
            blobbuf[pi].w = {id(csem): (csem, csem.n)}
            prev = csem

    def mm(out_ap, lhsT, rhs, start, stop):
        return nc.tensor.matmul(out_ap, lhsT, rhs, start=start, stop=stop)

    def _reg(tag):
        if PROF_REGIONS:
            REGIONS.append((tag, int(nc.get_next_instruction_name().split("-")[1])))

    from concourse.hw_specs import get_activation_tables
    _tabs = list(get_activation_tables(nc.m.arch).values())

    def _set_id(funcs):
        for _i, _fs in enumerate(_tabs):
            if all(f in _fs for f in funcs):
                return _i
        raise ValueError(funcs)

    SET_IDS = {
        "sigmoid": _set_id([AF.Sigmoid, AF.Square, AF.Copy]),
        "nlexp": _set_id([AF.Ln, AF.Exp, AF.Abs, AF.Relu, AF.Square, AF.Copy]),
        "silu": _set_id([AF.Silu, AF.Square, AF.Copy]),
    }
    cur_set = [None]

    def use_set(name):
        if cur_set[0] != name:
            nc.scalar.add_instruction(mybir.InstLoadActFuncSet(
                name=nc.get_next_instruction_name(),
                act_func_set_id=SET_IDS[name], ins=[], outs=[]))
            cur_set[0] = name

    lseq = [0]
    def load_piece(pi):
        sl = lseq[0] % 2
        lseq[0] += 1
        sbf = slotbuf[sl]
        nm, boff, ncols = _BP[pi]
        qW.dma(ring[:, sl * PIECE:sl * PIECE + ncols], BLOB[:, boff:boff + ncols],
               writes=[sbf], reads=[blobbuf[pi]])
        return sl * PIECE, sbf

    MAGIC = 0x5f3759df

    def rsqrt_magic(dst, pre, n):
        y = mgB[:, 0:n]
        h = mgC[:, 0:n]
        pre_read(V, [B("mg")])
        selfedge(V, nc.vector.tensor_scalar(y.bitcast(mybir.dt.int32),
                                            pre.bitcast(mybir.dt.int32), 1, 0,
                                            OP.logical_shift_right,
                                            OP.logical_shift_right))
        selfedge(V, nc.vector.tensor_scalar(y.bitcast(mybir.dt.int32),
                                            y.bitcast(mybir.dt.int32), -1, MAGIC,
                                            OP.mult, OP.add))
        for _ in range(2):
            selfedge(V, nc.vector.tensor_tensor(h, y, y, OP.mult))
            selfedge(V, nc.vector.tensor_tensor(h, h, pre, OP.mult))
            selfedge(V, nc.vector.tensor_scalar(h, h, -0.5, 1.5, OP.mult, OP.add))
            selfedge(V, nc.vector.tensor_tensor(y, y, h, OP.mult))
        i = nc.vector.tensor_copy(dst, y)
        return i

    def ckpt(n):
        if PHASE <= n:
            raise _Stop()

    def tap(tag, ap, bufs, cols):
        if DUMP != tag:
            return
        qX.dma(DBG[:, 0:cols], ap, reads=bufs)
        raise _Stop()

    try:
        def prologue_loads(s):
            r0 = 512 * s
            sso = (s % 2) * NCTX
            xbufs = [B(f"x{ct}") for ct in range(5)]
            for ct in range(5):
                qX.dma(x_sb[:, ct * D:(ct + 1) * D],
                       X[r0 + 128 * ct:r0 + 128 * ct + 128, :], writes=[xbufs[ct]])
            ccb = B("cc")
            qX.dma(cc_s[:, sso:sso + NCTX], CCT[:, r0:r0 + NCTX], writes=[ccb])
            qX.dma_add(ss_s[:, sso:sso + NCTX], SSTd[:, r0:r0 + NCTX], writes=[ccb])
            return xbufs, ccb

        def prologue(s, pre=None):
            if pre is None:
                pre = prologue_loads(s)
            xbufs, ccb = pre
            lnxTb = B("lnxT")
            for ct in range(5):
                xb = xbufs[ct]
                pre_read(V, [xb])
                pre_write(V, [B("st6"), B("st2"), B("stat"), B("mg")])
                nc.vector.bn_stats(st6[:, 0:6], x_sb[:, ct * D:ct * D + 512])
                selfedge(V, nc.vector.bn_stats(st6[:, 6:12],
                                               x_sb[:, ct * D + 512:ct * D + 1024]))
                selfedge(V, nc.vector.bn_aggr(st2[:], st6[:]))
                nc.vector.tensor_copy(stat[:, 4 * ct:4 * ct + 1], st2[:, 0:1])
                i = nc.vector.tensor_scalar(mgA[:, ct:ct + 1], st2[:, 1:2],
                                            1.0, EPS_LN, OP.mult, OP.add)
                post(V, i, writes=[B("st6"), B("st2"), B("stat"), B("mg")], reads=[xb])
            pre_write(V, [B("stat"), B("mg")])
            selfedge(V, rsqrt_magic(
                stat[:].rearrange("p (c f) -> p c f", f=4)[:, :, 1:2], mgA[:, 0:5], 5))
            selfedge(V, nc.vector.tensor_tensor(
                stat[:].rearrange("p (c f) -> p c f", f=4)[:, :, 2:3],
                stat[:].rearrange("p (c f) -> p c f", f=4)[:, :, 0:1],
                stat[:].rearrange("p (c f) -> p c f", f=4)[:, :, 1:2], OP.mult))
            i = nc.vector.tensor_scalar(
                stat[:].rearrange("p (c f) -> p c f", f=4)[:, :, 2:3],
                stat[:].rearrange("p (c f) -> p c f", f=4)[:, :, 2:3],
                -1.0, 0.0, OP.mult, OP.add)
            post(V, i, writes=[B("stat"), B("mg")])
            xslot = [(xln, 0, B("xln0")), (xln, D, B("xln1")),
                     (arn, A_tau, B("tau0")), (arn, A_tau + D, B("tau1")),
                     (arn, A_mtk, B("mtk"))]
            for ct in range(5):
                t, xo, xlb = xslot[ct]
                pre_read(V, [xbufs[ct], B("stat")])
                pre_write(V, [xlb])
                i = nc.vector.tensor_scalar(t[:, xo:xo + D], x_sb[:, ct * D:(ct + 1) * D],
                                            stat[:, 4 * ct + 1:4 * ct + 2],
                                            stat[:, 4 * ct + 2:4 * ct + 3],
                                            OP.mult, OP.add)
                post(V, i, writes=[xlb], reads=[xbufs[ct], B("stat")])
            for ct in range(5):
                t, xo, xlb = xslot[ct]
                pre_read(PE, [xlb, B("c_eye")])
                pre_write(PE, [pb(pTR)])
                for ch in range(8):
                    i = nc.tensor.transpose(pTR[:, 128 * ch:128 * ch + 128],
                                            t[:, xo + 128 * ch:xo + 128 * ch + 128],
                                            c_eye[:])
                post(PE, i, writes=[pb(pTR)], reads=[xlb, B("c_eye")])
                pre_read(V, [pb(pTR)])
                pre_write(V, [lnxTb])
                i = nc.vector.tensor_copy(
                    arn[:, A_lnxT:A_lnxT + 8 * NCTX].rearrange(
                        "p (k c) -> p k c", k=8)[:, :, 128 * ct:128 * ct + 128],
                    pTR[:].rearrange("p (k c) -> p k c", k=8))
                post(V, i, writes=[lnxTb], reads=[pb(pTR)])
            return xbufs, ccb, lnxTb

        _reg("init")
        _pre0 = prologue_loads(0)
        emit_casts()
        STV = {}
        STV[0] = prologue(0, _pre0)
        for s in range(NSTI):
            r0 = 512 * s
            sso = (s % 2) * NCTX

            _reg(f"s{s}:led")
            wqo, wqb = load_piece(0)
            wko, wkb = load_piece(1)
            xbufs, ccb, lnxTb = STV[s]

            tap("lnxT", arn[:, A_lnxT:A_lnxT + 5120], [lnxTb], 5120)
            tap("stat", stat[:].rearrange("p (c f) -> p c f", f=4)[:, :, 1:3], [B("stat")], 10)
            ckpt(2)
            qrob, krob, vb = B("qro"), B("kro"), B("vv")

            def stageA(ot):
                B0 = 8 * s + 2 * ot
                qdb = B("qd")
                for blk in range(2):
                    Bg = B0 + blk
                    qdo = A_qd + 1024 * blk
                    toko = 64 * (Bg - 8 * s - 2)
                    qW.dma_add(
                        arn[0:64, qdo:qdo + 1024].rearrange(
                            "p (g c) -> p g c", g=8)[:, :, 0:64],
                        arn[0:64, A_qro:A_qro + 4096].rearrange(
                            "p (g c) -> p g c", g=8)[:, :, toko:toko + 64],
                        writes=[qdb], reads=[qrob])
                    qW.dma_add(
                        arn[64:128, qdo:qdo + 1024].rearrange(
                            "p (g c) -> p g c", g=8)[:, :, 64:128],
                        arn[64:128, A_qro:A_qro + 4096].rearrange(
                            "p (g c) -> p g c", g=8)[:, :, toko:toko + 64],
                        writes=[qdb], reads=[qrob])

            def stageB_mms(ot, blk):
                Bg = 8 * s + 2 * ot + blk
                c0 = 64 * (Bg - 1 - 8 * s)
                pl, ph = (pC, pD) if blk == 0 else (pE, pF)
                qdo = A_qd + 1024 * blk
                pre_read(PE, [krob, B("qd")])
                pre_write(PE, [pb(pl), pb(ph)])
                for p in range(8):
                    rh = arn[:, qdo + 128 * p:qdo + 128 * p + 128]
                    pt = pl if p < 4 else ph
                    oc = 128 * (p % 4)
                    lt = arn[:, A_kro + p * NCTX + c0:A_kro + p * NCTX + c0 + 128]
                    i = mm(pt[:, oc:oc + 128], lt, rh, True, True)
                post(PE, i, writes=[pb(pl), pb(ph)], reads=[krob, B("qd")])

            def stageB_act(ot, blk):
                Bg = 8 * s + 2 * ot + blk
                pl, ph = (pC, pD) if blk == 0 else (pE, pF)
                tauo = A_tau + 1024 * blk
                taub = B(f"tau{blk}")
                pre_read(A, [pb(pl), pb(ph)])
                pre_write(A, [taub])
                nc.scalar.activation(arn[:, tauo:tauo + 512], pl[:],
                                     AF.Sigmoid, scale=c_im[:, 7:8])
                i = nc.scalar.activation(arn[:, tauo + 512:tauo + 1024], ph[:],
                                         AF.Sigmoid, scale=c_im[:, 7:8])
                post(A, i, writes=[taub], reads=[pb(pl), pb(ph)])
                mk, mkb = (c_mf, B("c_mf")) if Bg == 2 else (c_mm, B("c_mm"))
                pre_read(G, [taub, mkb])
                pre_write(G, [taub])
                i = nc.gpsimd.tensor_tensor(arn[:, tauo:tauo + 1024],
                                            arn[:, tauo:tauo + 1024], mk[:],
                                            OP.mult)
                post(G, i, writes=[taub], reads=[mkb])

            def stageB(ot, blks=(0, 1)):
                for blk in blks:
                    stageB_mms(ot, blk)
                    stageB_act(ot, blk)

            def proj_rope(kind, wo_, wb_, dsto, dstb, fillers=()):
                ncols = 512 if kind == "q" else 640
                base = 128 if kind == "q" else 0

                qtp_offs = (A_qtp, A_qtp + NCTX, A_dg)

                def stage1(ch):
                    acc = pA if ch % 2 == 0 else pC
                    tail = ncols == 640
                    pre_read(PE, [lnxTb, wb_])
                    pre_write(PE, [pb(acc)] + ([pb(pM)] if tail else []))
                    for k in range(8):
                        lt = ring[:, wo_ + k * 1024 + 128 * ch:wo_ + k * 1024 + 128 * ch + 128]
                        rh = arn[:, A_lnxT + k * NCTX + base:A_lnxT + k * NCTX + base + 512]
                        i = mm(acc[:], lt, rh, k == 0, k == 7)
                    if tail:
                        for k in range(8):
                            lt = ring[:, wo_ + k * 1024 + 128 * ch:wo_ + k * 1024 + 128 * ch + 128]
                            rh = arn[:, A_lnxT + k * NCTX + 512:A_lnxT + k * NCTX + 640]
                            i = mm(pM[:, 0:128] if ch % 2 == 0 else pM[:, 256:384],
                                   lt, rh, k == 0, k == 7)
                    post(PE, i, writes=[pb(acc)] + ([pb(pM)] if tail else []),
                         reads=[lnxTb, wb_])
                    qtp_o = qtp_offs[ch % 3]
                    qtb = B(f"qtp{ch % 3}")
                    pre_read(A, [pb(acc)] + ([pb(pM)] if tail else []))
                    pre_write(A, [qtb])
                    i = nc.scalar.copy(arn[:, qtp_o:qtp_o + 512], acc[:])
                    if tail:
                        i = nc.scalar.copy(arn[:, qtp_o + 512:qtp_o + 640],
                                           pM[:, 0:128] if ch % 2 == 0 else pM[:, 256:384])
                    post(A, i, writes=[qtb],
                         reads=[pb(acc)] + ([pb(pM)] if tail else []))

                def stage2(ch):
                    tail = ncols == 640
                    rot = pB if ch % 2 == 0 else pD
                    qtp_o = qtp_offs[ch % 3]
                    t1o = A_t1 + (ch % 2) * NCTX
                    t2o = A_t2 + (ch % 2) * NCTX
                    qtb = B(f"qtp{ch % 3}")
                    t1b = B(f"t1_{ch % 2}")
                    t2b = B(f"t2_{ch % 2}")
                    pre_read(PE, [qtb, B("c_prot")])
                    pre_write(PE, [pb(rot)] + ([pb(pM)] if tail else []))
                    i = mm(rot[:], c_prot[:], arn[:, qtp_o:qtp_o + 512], True, True)
                    if tail:
                        i = mm(pM[:, 128:256] if ch % 2 == 0 else pM[:, 384:512],
                               c_prot[:], arn[:, qtp_o + 512:qtp_o + 640], True, True)
                    post(PE, i, writes=[pb(rot)] + ([pb(pM)] if tail else []),
                         reads=[qtb, B("c_prot")])
                    pre_read(V, [qtb, ccb])
                    pre_write(V, [t1b])
                    i = nc.vector.tensor_tensor(
                        arn[:, t1o:t1o + ncols], arn[:, qtp_o:qtp_o + ncols],
                        cc_s[:, sso + base:sso + base + ncols], OP.mult)
                    post(V, i, writes=[t1b], reads=[qtb, ccb])
                    pre_read(V, [pb(rot), ccb] + ([pb(pM)] if tail else []))
                    pre_write(V, [t2b])
                    i = nc.vector.tensor_tensor(arn[:, t2o:t2o + 512], rot[:],
                                                ss_s[:, sso + base:sso + base + 512], OP.mult)
                    if tail:
                        i = nc.vector.tensor_tensor(
                            arn[:, t2o + 512:t2o + 640],
                            pM[:, 128:256] if ch % 2 == 0 else pM[:, 384:512],
                            ss_s[:, sso + base + 512:sso + base + 640], OP.mult)
                    post(V, i, writes=[t2b],
                         reads=[pb(rot), ccb] + ([pb(pM)] if tail else []))
                    pre_read(V, [t1b, t2b])
                    pre_write(V, [dstb])
                    i = nc.vector.tensor_tensor(
                        arn[:, dsto + ch * ncols:dsto + (ch + 1) * ncols],
                        arn[:, t1o:t1o + ncols], arn[:, t2o:t2o + ncols], OP.add)
                    post(V, i, writes=[dstb], reads=[t1b, t2b])

                fill = list(fillers)
                for ch in range(8):
                    stage1(ch)
                    if ch >= 2:
                        stage2(ch - 2)
                        if fill:
                            fill.pop(0)()
                stage2(6)
                stage2(7)
                while fill:
                    fill.pop(0)()

            _reg(f"s{s}:qkv_q")
            proj_rope("q", wqo, wqb, A_qro, qrob)
            wvo, wvb = load_piece(2)
            stageA(1)
            v64b = B("v64")

            def vtc(tc):
                for bk in range(2):
                    pt = pE if bk == 0 else pF
                    pre_read(PE, [lnxTb, wvb])
                    pre_write(PE, [pb(pt)])
                    for k in range(8):
                        lt = arn[:, A_lnxT + k * NCTX + 128 * tc:A_lnxT + k * NCTX + 128 * tc + 128]
                        rh = ring[:, wvo + k * 1024 + 512 * bk:wvo + k * 1024 + 512 * bk + 512]
                        i = mm(pt[:], lt, rh, k == 0, k == 7)
                    post(PE, i, writes=[pb(pt)], reads=[lnxTb, wvb])
                    if bk == 0:
                        pre_read(A, [pb(pt)])
                        pre_write(A, [vb])
                        i = nc.scalar.activation(
                            arn[:, A_v + 1024 * tc:A_v + 1024 * tc + 512],
                            pt[:], AF.Copy)
                        post(A, i, writes=[vb], reads=[pb(pt)])
                    else:
                        pre_read(V, [pb(pt)])
                        pre_write(V, [vb])
                        i = nc.vector.tensor_copy(
                            arn[:, A_v + 1024 * tc + 512:A_v + 1024 * tc + 1024],
                            pt[:])
                        post(V, i, writes=[vb], reads=[pb(pt)])
                if tc >= 1:
                    j = tc - 1
                    if j == 0:
                        qX.dma(arn[0:64, A_v64 + 1024 * j:A_v64 + 1024 * j + 1024],
                               arn[64:128, A_v + 1024 * j:A_v + 1024 * j + 1024],
                               writes=[v64b], reads=[vb])
                    else:
                        qX.dma_add(arn[0:64, A_v64 + 1024 * j:A_v64 + 1024 * j + 1024],
                                   arn[64:128, A_v + 1024 * j:A_v + 1024 * j + 1024],
                                   writes=[v64b], reads=[vb])
                    qX.dma_add(arn[64:128, A_v64 + 1024 * j:A_v64 + 1024 * j + 1024],
                               arn[0:64, A_v + 1024 * tc:A_v + 1024 * tc + 1024],
                               writes=[v64b], reads=[vb])

            _reg(f"s{s}:qkv_k")
            proj_rope("k", wko, wkb, A_kro, krob,
                      fillers=[lambda t=t: vtc(t) for t in range(5)])
            tap("qro", arn[:, A_qro:A_qro + 4096], [qrob], 4096)
            tap("kro", arn[:, A_kro:A_kro + 5120], [krob], 5120)
            ckpt(3)
            woo, wob = load_piece(3)
            use_set("sigmoid")
            stageB_mms(1, 0)
            rw1h = [None] * 4
            rw1h[0] = load_piece(4)
            _reg(f"s{s}:qkv_v")
            stageB_act(1, 0)
            stageB_mms(1, 1)
            stageB_act(1, 1)

            ckpt(4)
            tap("vv", arn[:, A_v:A_v + 5120], [vb], 5120)

            tap("v64", arn[:, A_v64:A_v64 + 4096], [v64b], 4096)

            ln2Tb = B("ln2T")

            def epilogue_pre(ot):
                tc = ot - 1
                y1b = B(f"y1_{ot}")
                zb = B(f"z{ot}")
                pre_read(V, [B("zss")])
                pre_write(V, [B("mg"), B("zrs")])
                selfedge(V, nc.vector.tensor_scalar(mgA[:, 6:7], zss[:, tc:tc + 1],
                                                    1.0 / D, EPS_RMS, OP.mult, OP.add))
                i = rsqrt_magic(zrs[:, tc:tc + 1], mgA[:, 6:7], 1)
                post(V, i, writes=[B("mg"), B("zrs")], reads=[B("zss")])
                pre_read(V, [zb, B("zrs"), B("c_scale")])
                pre_write(V, [zb])
                i = nc.vector.scalar_tensor_tensor(
                    z_sb[:, tc * D:ot * D], z_sb[:, tc * D:ot * D],
                    zrs[:, tc:tc + 1], c_scale[:], OP.mult, OP.mult)
                post(V, i, writes=[zb], reads=[B("zrs"), B("c_scale")])
                pre_read(G, [zb, xbufs[ot]])
                pre_write(G, [y1b])
                i = nc.gpsimd.tensor_tensor(
                    y1[:, tc * D:ot * D], z_sb[:, tc * D:ot * D],
                    x_sb[:, ot * D:(ot + 1) * D], OP.add)
                post(G, i, writes=[y1b], reads=[zb, xbufs[ot]])
                pre_read(V, [y1b])
                pre_write(V, [B("st6"), B("st2"), B("stat"), B("mg")])
                nc.vector.bn_stats(st6[:, 0:6], y1[:, tc * D:tc * D + 512])
                selfedge(V, nc.vector.bn_stats(st6[:, 6:12],
                                               y1[:, tc * D + 512:tc * D + 1024]))
                selfedge(V, nc.vector.bn_aggr(st2[:], st6[:]))
                selfedge(V, nc.vector.tensor_scalar(mgA[:, 7:8], st2[:, 1:2],
                                                    1.0, EPS_LN, OP.mult, OP.add))
                selfedge(V, rsqrt_magic(stat[:, 4 * tc + 1:4 * tc + 2], mgA[:, 7:8], 1))
                i = nc.vector.tensor_scalar(stat[:, 4 * tc + 2:4 * tc + 3], st2[:, 0:1],
                                            stat[:, 4 * tc + 1:4 * tc + 2], -1.0,
                                            OP.mult, OP.mult)
                post(V, i, writes=[B("st6"), B("st2"), B("stat"), B("mg")], reads=[y1b])
                xlb = B(f"xln{tc % 2}")
                xo = (tc % 2) * D
                pre_read(V, [y1b, B("stat")])
                pre_write(V, [xlb])
                i = nc.vector.tensor_scalar(xln[:, xo:xo + D], y1[:, tc * D:(tc + 1) * D],
                                            stat[:, 4 * tc + 1:4 * tc + 2],
                                            stat[:, 4 * tc + 2:4 * tc + 3],
                                            OP.mult, OP.add)
                post(V, i, writes=[xlb], reads=[y1b, B("stat")])

            def epilogue_post(ot):
                tc = ot - 1
                xlb = B(f"xln{tc % 2}")
                xo = (tc % 2) * D
                pre_read(PE, [xlb, B("c_eye")])
                pre_write(PE, [pb(pTR)])
                for ch in range(8):
                    i = nc.tensor.transpose(pTR[:, 128 * ch:128 * ch + 128],
                                            xln[:, xo + 128 * ch:xo + 128 * ch + 128],
                                            c_eye[:])
                post(PE, i, writes=[pb(pTR)], reads=[xlb, B("c_eye")])
                hb = B("ln2Th0") if tc <= 1 else B("ln2Th1")
                pre_read(V, [pb(pTR)])
                pre_write(V, [ln2Tb, hb])
                i = nc.vector.tensor_copy(
                    arn[:, F_ln2T:F_ln2T + 4096].rearrange(
                        "p (k c) -> p k c", k=8)[:, :, 128 * tc:128 * tc + 128],
                    pTR[:].rearrange("p (k c) -> p k c", k=8))
                post(V, i, writes=[ln2Tb, hb], reads=[pb(pTR)])

            def routing_mm(tc):
                pre_read(PE, [ln2Tb, B("c_rt"), B("c_or"), B("c_eb")])
                pre_write(PE, [pb(pM)])
                for k in range(8):
                    lt = arn[:, F_ln2T + k * 512 + 128 * tc:F_ln2T + k * 512 + 128 * tc + 128]
                    mm(pM[:, 384:392], lt, c_rt[:, 8 * k:8 * k + 8], k == 0, False)
                i = mm(pM[:, 384:392], c_or[:], c_eb[:], False, True)
                post(PE, i, writes=[pb(pM)],
                     reads=[ln2Tb, B("c_rt"), B("c_or"), B("c_eb")])
                pre_read(A, [pb(pM)])
                pre_write(A, [B("logi")])
                i = nc.scalar.activation(logi[:, 8 * tc:8 * tc + 8], pM[:, 384:392],
                                         AF.Copy)
                post(A, i, writes=[B("logi")], reads=[pb(pM)])

            def top2_pair(tcs):
                use_set("nlexp")
                n = 8 * len(tcs)
                o = 8 * tcs[0]
                pre_read(A, [B("logi")])
                pre_write(A, [B("rsc1"), B("rsc2")])
                selfedge(A, nc.scalar.activation(rsc1[:, o:o + n],
                                                 logi[:, o:o + n], AF.Abs))
                selfedge(A, nc.scalar.activation(rsc1[:, o:o + n], rsc1[:, o:o + n],
                                                 AF.Exp, scale=c_im[:, 5:6]))
                selfedge(A, nc.scalar.activation(rsc1[:, o:o + n], rsc1[:, o:o + n],
                                                 AF.Ln, bias=c_im[:, 1:2]))
                i = nc.scalar.activation(rsc2[:, o:o + n], logi[:, o:o + n], AF.Relu)
                post(A, i, writes=[B("rsc1"), B("rsc2")], reads=[B("logi")])
                pre_read(V, [B("rsc1"), B("rsc2")])
                pre_write(V, [B("rsc1")])
                i = nc.vector.tensor_tensor(rsc1[:, o:o + n], rsc1[:, o:o + n],
                                            rsc2[:, o:o + n], OP.add)
                post(V, i, writes=[B("rsc1")], reads=[B("rsc2")])
                pre_read(A, [B("rsc1")])
                pre_write(A, [B("rsc2")])
                selfedge(A, nc.scalar.activation(rsc2[:, o:o + n], rsc1[:, o:o + n],
                                                 AF.Ln))
                i = nc.scalar.activation(rsc2[:, o:o + n], rsc2[:, o:o + n],
                                         AF.Exp, scale=c_im[:, 6:7])
                post(A, i, writes=[B("rsc2")], reads=[B("rsc1")])
                for tc in tcs:
                    c = 8 * tc
                    pre_read(V, [B("rsc2")])
                    pre_write(V, [B("rsc3"), B("rsc1"), B("cf")])
                    selfedge(V, nc.vector.reduce_max(rsc3[:, 0:1], rsc2[:, c:c + 8],
                                                     mybir.AxisListType.X))
                    selfedge(V, nc.vector.tensor_scalar(
                        rsc1[:, c:c + 8], rsc2[:, c:c + 8], rsc3[:, 0:1],
                        -1e9, OP.is_equal, OP.mult))
                    selfedge(V, nc.vector.tensor_tensor(
                        rsc1[:, c:c + 8], rsc1[:, c:c + 8], rsc2[:, c:c + 8], OP.add))
                    selfedge(V, nc.vector.reduce_max(rsc3[:, 1:2], rsc1[:, c:c + 8],
                                                     mybir.AxisListType.X))
                    selfedge(V, nc.vector.tensor_scalar(
                        rsc1[:, c:c + 8], rsc2[:, c:c + 8], rsc3[:, 1:2],
                        0.0, OP.is_ge, OP.add))
                    i = nc.vector.tensor_tensor(cf_sb[:, 9 * tc:9 * tc + 8],
                                                rsc2[:, c:c + 8], rsc1[:, c:c + 8],
                                                OP.mult)
                    post(V, i, writes=[B("rsc3"), B("rsc1"), B("cf")],
                         reads=[B("rsc2")])

            def w1_half(src, src_off, srcb, dst_off, dstb, bias_ap, biasb, h,
                        banks, ltok):
                for m in range(4):
                    pt = banks[m]
                    pre_read(PE, [srcb, ltok, biasb])
                    pre_write(PE, [pb(pt)])
                    for k in range(8):
                        lt = src[:, src_off + k * 512 + 128 * m:src_off + k * 512 + 128 * m + 128]
                        rh = arn[:, F_ln2T + k * 512 + 256 * h:F_ln2T + k * 512 + 256 * h + 256]
                        i = mm(pt[:, 0:256], lt, rh, k == 0, k == 7)
                    post(PE, i, writes=[pb(pt)], reads=[srcb, ltok, biasb])
                    pre_read(A, [pb(pt), biasb])
                    pre_write(A, [dstb])
                    i = nc.scalar.activation(
                        arn[:, dst_off + 512 * m + 256 * h:dst_off + 512 * m + 256 * h + 256],
                        pt[:, 0:256], AF.Silu, bias=bias_ap[:, m:m + 1])
                    post(A, i, writes=[dstb], reads=[pb(pt), biasb])



            def stageC(ot, blk):
                B0 = 8 * s + 2 * ot
                Bg = B0 + blk
                mbase = 256 + 16 * (ot % 2)
                mtkb = B("mtk")
                msgTb = B("msgT")
                taub = B(f"tau{blk}")
                tauo = A_tau + 1024 * blk
                pl, ph = (pC, pD) if blk == 0 else (pE, pF)
                pre_read(PE, [taub, vb, B("v64"), B("c_oc")])
                pre_write(PE, [pb(pl), pb(ph), pb(pM)])
                for p in range(8):
                    pt = pl if p < 4 else ph
                    oc = 128 * (p % 4)
                    mc = mbase + 8 * blk + p
                    lt = arn[:, tauo + 128 * p:tauo + 128 * p + 128]
                    if Bg % 2 == 1:
                        vt = (Bg - 1) // 2 - 4 * s
                        rh = arn[:, A_v + 1024 * vt + 128 * p:A_v + 1024 * vt + 128 * p + 128]
                    else:
                        j = (Bg - 8 * s - 2) // 2
                        rh = arn[:, A_v64 + 1024 * j + 128 * p:A_v64 + 1024 * j + 128 * p + 128]
                    i = mm(pt[:, oc:oc + 128], lt, rh, True, True)
                    i = mm(pM[:, mc:mc + 1], lt, c_oc[:], True, True)
                post(PE, i, writes=[pb(pl), pb(ph), pb(pM)],
                     reads=[taub, vb, B("v64"), B("c_oc")])
                mo = A_mtk + 1024 * blk
                pre_read(V, [pb(pM)])
                pre_write(V, [B("rmas"), B("rm2")])
                selfedge(V, nc.vector.tensor_scalar(
                    rmas[:, 8 * blk:8 * blk + 8], pM[:, mbase + 8 * blk:mbase + 8 * blk + 8],
                    1e-6, 0.0, OP.max, OP.add))
                i = nc.vector.reciprocal(rm2[:, 8 * blk:8 * blk + 8],
                                         rmas[:, 8 * blk:8 * blk + 8])
                post(V, i, writes=[B("rmas"), B("rm2")], reads=[pb(pM)])
                pre_read(A, [pb(pl), pb(ph), B("rm2")])
                pre_write(A, [mtkb])
                for p in range(6):
                    idx = 8 * blk + p
                    pt = pl if p < 4 else ph
                    i = nc.scalar.activation(
                        arn[:, mo + 128 * p:mo + 128 * p + 128],
                        pt[:, 128 * (p % 4):128 * (p % 4) + 128], AF.Copy,
                        scale=rm2[:, idx:idx + 1])
                post(A, i, writes=[mtkb], reads=[pb(pl), pb(ph), B("rm2")])
                pre_read(V, [pb(ph), B("rm2")])
                pre_write(V, [mtkb])
                for p in range(6, 8):
                    idx = 8 * blk + p
                    i = nc.vector.tensor_scalar(
                        arn[:, mo + 128 * p:mo + 128 * p + 128],
                        ph[:, 128 * (p % 4):128 * (p % 4) + 128],
                        rm2[:, idx:idx + 1], 0.0, OP.mult, OP.add)
                post(V, i, writes=[mtkb], reads=[pb(ph), B("rm2")])
                pre_read(PE, [mtkb, B("c_eye")])
                pre_write(PE, [pb(pl), pb(ph)])
                for p in range(8):
                    pt = pl if p < 4 else ph
                    oc = 128 * (p % 4)
                    i = mm(pt[:, oc:oc + 128],
                           arn[:, mo + 128 * p:mo + 128 * p + 128],
                           c_eye[:], True, True)
                post(PE, i, writes=[pb(pl), pb(ph)], reads=[mtkb, B("c_eye")])
                pre_read(A, [pb(pl), pb(ph)])
                pre_write(A, [msgTb])
                for par, (q0, q1) in enumerate(((0, 64), (64, 128))):
                    nc.scalar.copy(
                        arn[q0:q1, A_msgT:A_msgT + 512].rearrange(
                            "p (g c) -> p g c", g=4)[:, :, 64 * blk:64 * blk + 64],
                        pl[q0:q1, :].rearrange(
                            "p (g c) -> p g c", g=4)[:, :, q0:q0 + 64])
                    i = nc.scalar.copy(
                        arn[q0:q1, A_msgT + 512:A_msgT + 1024].rearrange(
                            "p (g c) -> p g c", g=4)[:, :, 64 * blk:64 * blk + 64],
                        ph[q0:q1, :].rearrange(
                            "p (g c) -> p g c", g=4)[:, :, q0:q0 + 64])
                post(A, i, writes=[msgTb], reads=[pb(pl), pb(ph)])

            def stageW(ot):
                msgTb = B("msgT")
                pre_read(PE, [msgTb, wob])
                pre_write(PE, [pb(pA), pb(pB)])
                for bk in range(2):
                    pt = pA if bk == 0 else pB
                    for ch in range(8):
                        lt = arn[:, A_msgT + 128 * ch:A_msgT + 128 * ch + 128]
                        rh = ring[:, woo + ch * 1024 + 512 * bk:woo + ch * 1024 + 512 * bk + 512]
                        i = mm(pt[:], lt, rh, ch == 0, ch == 7)
                post(PE, i, writes=[pb(pA), pb(pB)], reads=[msgTb, wob])

            def stageZ1(ot):
                ct = ot
                zb = B(f"z{ot}")
                pre_read(A, [pb(pA), pb(pB)])
                pre_write(A, [zb])
                nc.scalar.copy(z_sb[:, (ot - 1) * D:(ot - 1) * D + 512], pA[:])
                i = nc.scalar.copy(z_sb[:, (ot - 1) * D + 512:ot * D], pB[:])
                post(A, i, writes=[zb], reads=[pb(pA), pb(pB)])
                pre_read(V, [zb, xbufs[ct], B("stat")])
                pre_write(V, [zb])
                selfedge(V, nc.vector.scalar_tensor_tensor(
                    z_sb[:, (ot - 1) * D:ot * D],
                    x_sb[:, ct * D:(ct + 1) * D],
                    stat[:, 4 * ct + 1:4 * ct + 2],
                    z_sb[:, (ot - 1) * D:ot * D], OP.mult, OP.add))
                i = nc.vector.tensor_scalar(
                    z_sb[:, (ot - 1) * D:ot * D], z_sb[:, (ot - 1) * D:ot * D],
                    stat[:, 4 * ct + 2:4 * ct + 3], 0.0, OP.add, OP.add)
                post(V, i, writes=[zb], reads=[xbufs[ct], B("stat")])

            def stageZ2(ot):
                zb = B(f"z{ot}")
                pre_read(A, [zb])
                pre_write(A, [B("zss"), B("trash")])
                i = nc.scalar.activation(trash[:], z_sb[:, (ot - 1) * D:ot * D],
                                         AF.Square, accum_out=zss[:, ot - 1:ot])
                post(A, i, writes=[B("zss"), B("trash")], reads=[zb])

            _reg(f"s{s}:attn")
            for ot in range(1, 5):
                _reg(f"s{s}:at{ot}")
                stageC(ot, 0)
                if ot >= 2:
                    stageZ2(ot - 1)
                if ot >= 3:
                    epilogue_post(ot - 2)
                stageC(ot, 1)
                if ot < 4:
                    stageA(ot + 1)
                    stageB(ot + 1)
                else:
                    use_set("silu")
                    routing_mm(0)
                    routing_mm(1)
                stageW(ot)
                if ot >= 2:
                    epilogue_pre(ot - 1)
                if ot == 4:
                    rw1h[1] = load_piece(5)
                    w1_half(c_sw1, 0, B("c_sw1"), F_sh, B("sh"), c_sb1,
                            B("c_sb1"), 0, (pC, pD, pE, pF), B("ln2Th0"))
                    w1_half(ring, rw1h[0][0], rw1h[0][1], F_g[0], B("g0"),
                            c_rb1[:, 0:4], B("c_rb1"), 0, (pC, pD, pE, pF),
                            B("ln2Th0"))
                    w1_half(ring, rw1h[0][0] + 4096, rw1h[0][1], F_g[1], B("g1"),
                            c_rb1[:, 4:8], B("c_rb1"), 0, (pC, pD, pE, pF),
                            B("ln2Th0"))
                stageZ1(ot)
            _reg(f"s{s}:epi")
            stageZ2(4)
            epilogue_post(3)
            routing_mm(2)
            epilogue_pre(4)
            epilogue_post(4)
            routing_mm(3)
            w1_half(ring, rw1h[1][0], rw1h[1][1], F_g[2], B("g2"),
                    c_rb1[:, 8:12], B("c_rb1"), 0, (pC, pD, pE, pF),
                    B("ln2Th0"))
            w1_half(ring, rw1h[1][0] + 4096, rw1h[1][1], F_g[3], B("g3"),
                    c_rb1[:, 12:16], B("c_rb1"), 0, (pC, pD, pE, pF),
                    B("ln2Th0"))
            top2_pair((0, 1, 2, 3))
            tap("z", z_sb[:], [B("z1"), B("z2"), B("z3"), B("z4")], 4096)
            ckpt(5)
            tap("y1", y1[:], [B("y1_1"), B("y1_2"), B("y1_3"), B("y1_4")], 4096)
            tap("ln2T", arn[:, F_ln2T:F_ln2T + 4096], [ln2Tb], 4096)
            tap("cf", cf_sb[:], [B("cf")], 36)
            ckpt(6)
            _reg(f"s{s}:w1sh")
            use_set("silu")

            def w1_block(src, src_off, srcb, dst_off, dstb, bias_ap, biasb):
                for m in range(4):
                    pt = (pA, pB, pD, pE)[m]
                    pre_read(PE, [srcb, ln2Tb])
                    pre_write(PE, [pb(pt)])
                    for k in range(8):
                        lt = src[:, src_off + k * 512 + 128 * m:src_off + k * 512 + 128 * m + 128]
                        rh = arn[:, F_ln2T + k * 512:F_ln2T + k * 512 + 512]
                        i = mm(pt[:], lt, rh, k == 0, k == 7)
                    post(PE, i, writes=[pb(pt)], reads=[srcb, ln2Tb])
                    pre_read(A, [pb(pt), biasb])
                    pre_write(A, [dstb])
                    i = nc.scalar.activation(
                        arn[:, dst_off + 512 * m:dst_off + 512 * m + 512], pt[:],
                        AF.Silu, bias=bias_ap[:, m:m + 1])
                    post(A, i, writes=[dstb], reads=[pb(pt), biasb])
            shb = B("sh")
            w1_half(c_sw1, 0, B("c_sw1"), F_sh, shb, c_sb1, B("c_sb1"), 1,
                    (pA, pB, pD, pE), ln2Tb)
            use_set("silu")
            gb = [B(f"g{e}") for e in range(8)]
            for e in range(8):
                _reg(f"s{s}:w1e{e}")
                pr = e // 2
                if e % 2 == 0 and pr > 0:
                    pass
                eo = (e % 2) * 4096
                rwo, rwb = rw1h[pr]
                if e <= 3:
                    w1_half(ring, rwo + eo, rwb, F_g[e], gb[e],
                            c_rb1[:, 4 * e:4 * e + 4], B("c_rb1"), 1,
                            (pA, pB, pD, pE), ln2Tb)
                else:
                    w1_block(ring, rwo + eo, rwb, F_g[e], gb[e],
                             c_rb1[:, 4 * e:4 * e + 4], B("c_rb1"))
                dcfb = B("dcf")
                pre_read(V, [B("cf"), B("c_eye")])
                pre_write(V, [dcfb])
                for tc in range(4):
                    i = nc.vector.tensor_scalar(
                        arn[:, F_dcf + 128 * tc:F_dcf + 128 * tc + 128], c_eye[:],
                        cf_sb[:, 9 * tc + e:9 * tc + e + 1], 0.0, OP.mult, OP.add)
                post(V, i, writes=[dcfb], reads=[B("cf"), B("c_eye")])
                cbcb = B("cbc")
                pre_read(PE, [dcfb, B("c_ones")])
                pre_write(PE, [pb(pC)])
                for tc in range(4):
                    i = mm(pC[:, 128 * tc:128 * tc + 128], c_ones[:],
                           arn[:, F_dcf + 128 * tc:F_dcf + 128 * tc + 128], True, True)
                post(PE, i, writes=[pb(pC)], reads=[dcfb, B("c_ones")])
                pre_read(V, [pb(pC)])
                pre_write(V, [B("cbc")])
                i = nc.vector.tensor_copy(arn[:, F_cbc:F_cbc + 512], pC[:])
                post(V, i, writes=[B("cbc")], reads=[pb(pC)])
                for m in range(4):
                    pre_read(G, [gb[e], B("cbc")])
                    pre_write(G, [gb[e]])
                    i = nc.gpsimd.tensor_tensor(
                        arn[:, F_g[e] + 512 * m:F_g[e] + 512 * m + 512],
                        arn[:, F_g[e] + 512 * m:F_g[e] + 512 * m + 512],
                        arn[:, F_cbc:F_cbc + 512], OP.mult)
                    post(G, i, writes=[gb[e]], reads=[B("cbc")])
                if e in (3, 5):
                    rw1h[e // 2 + 1] = load_piece(4 + e // 2 + 1)

            tap("sh", arn[:, F_sh:F_sh + 2048], [shb], 2048)
            tap("g0", arn[:, F_g[0]:F_g[0] + 2048], [B("g0")], 2048)
            tap("g7", arn[:, F_g[7]:F_g[7] + 2048], [B("g7")], 2048)
            ckpt(7)
            pre_read(V, [B("cf")])
            pre_write(V, [B("cfb")])
            i = nc.vector.tensor_copy(cfb[:], cf_sb[:])
            post(V, i, writes=[B("cfb")], reads=[B("cf")])
            pre_read(PE, [B("cfb"), B("c_eye")])
            pre_write(PE, [pb(pTR)])
            for tc in range(4):
                i = nc.tensor.transpose(pTR[0:9, 128 * tc:128 * tc + 128],
                                        cfb[:, 9 * tc:9 * tc + 9], c_eye[:])
            post(PE, i, writes=[pb(pTR)], reads=[B("cfb"), B("c_eye")])
            pre_read(V, [pb(pTR)])
            pre_write(V, [B("cfT")])
            i = nc.vector.tensor_copy(cfT[:], pTR[0:9, 0:512])
            post(V, i, writes=[B("cfT")], reads=[pb(pTR)])
            for grp in range(2):
                _reg(f"s{s}:w2g{grp}")
                tcs = (0, 1) if grp == 0 else (2, 3)
                pos = {tcs[0]: (pD, pE), tcs[1]: (pF, pC)}
                started = {}
                for tc in tcs:
                    pl, ph = pos[tc]
                    pre_read(PE, [shb, B("cfT"), B("c_rb2"), B("c_sw2")])
                    pre_write(PE, [pb(pl), pb(ph)])
                    for bk in range(2):
                        pt = pl if bk == 0 else ph
                        for m in range(4):
                            lt = arn[:, F_sh + m * 512 + 128 * tc:F_sh + m * 512 + 128 * tc + 128]
                            rh = c_sw2[:, m * 1024 + 512 * bk:m * 1024 + 512 * bk + 512]
                            i = mm(pt[:], lt, rh, m == 0, False)
                        i = mm(pt[:], cfT[:, 128 * tc:128 * tc + 128],
                               c_rb2[:, 512 * bk:512 * bk + 512], False, False)
                    post(PE, i, writes=[pb(pl), pb(ph)],
                         reads=[shb, B("cfT"), B("c_rb2"), B("c_sw2")])
                for pr in range(4):
                    rw2o, rw2b = load_piece(8 + pr)
                    for tc in tcs:
                        pl, ph = pos[tc]
                        pre_read(PE, [rw2b] + [gb[2 * pr], gb[2 * pr + 1]])
                        pre_write(PE, [pb(pl), pb(ph)])
                        for ei in range(2):
                            e = 2 * pr + ei
                            eo = 4096 * ei
                            for bk in range(2):
                                pt = pl if bk == 0 else ph
                                for m in range(4):
                                    lt = arn[:, F_g[e] + m * 512 + 128 * tc:
                                             F_g[e] + m * 512 + 128 * tc + 128]
                                    rh = ring[:, rw2o + eo + m * 1024 + 512 * bk:
                                              rw2o + eo + m * 1024 + 512 * bk + 512]
                                    last = pr == 3 and ei == 1 and m == 3
                                    i = mm(pt[:], lt, rh, False, last)
                        post(PE, i, writes=[pb(pl), pb(ph)],
                             reads=[rw2b, gb[2 * pr], gb[2 * pr + 1]])
                if grp == 1 and s + 1 < NSTI:
                    _reg(f"s{s}:prolog")
                    STV[s + 1] = prologue(s + 1)
                    _reg(f"s{s}:w2tail")
                _reg(f"s{s}:y1o{grp}")
                for tc in tcs:
                    pl, ph = pos[tc]
                    y1b = B(f"y1_{tc + 1}")
                    pre_read(V, [pb(pl), pb(ph)])
                    pre_write(V, [y1b])
                    nc.vector.tensor_tensor(y1[:, tc * D:tc * D + 512],
                                            y1[:, tc * D:tc * D + 512], pl[:], OP.add)
                    i = nc.vector.tensor_tensor(y1[:, tc * D + 512:(tc + 1) * D],
                                                y1[:, tc * D + 512:(tc + 1) * D],
                                                ph[:], OP.add)
                    post(V, i, writes=[y1b], reads=[pb(pl), pb(ph)])
                    orow = r0 + 128 * tc
                    qO.dma(OUT[orow:orow + 128, :], y1[:, tc * D:(tc + 1) * D],
                           reads=[y1b])


    except _Stop:
        pass
    for q in (qW, qX, qO):
        for ln in q.lanes:
            q.e.wait_ge(ln.h, ln.n)
    for ln in qO.lanes:
        nc.sync.wait_ge(ln.h, ln.n)
        nc.vector.wait_ge(ln.h, ln.n)
    stack.close()
    return nc


def host_in_maps(L0_attn_x, L0_attn_wq, L0_attn_wk, L0_attn_wv, L0_attn_wo,
                 L0_attn_norm_scale, L0_ffn_routing_W, L0_ffn_expert_bias,
                 L0_ffn_sW1, L0_ffn_sb1, L0_ffn_sW2, L0_ffn_sb2,
                 L0_ffn_rW1, L0_ffn_rb1, L0_ffn_rW2, L0_ffn_rb2, ncores=8):
    x = np.asarray(L0_attn_x, dtype=np.float32)
    nsc = np.asarray(L0_attn_norm_scale, np.float32)
    rb1 = np.asarray(L0_ffn_rb1, np.float32)
    sb1 = np.asarray(L0_ffn_sb1, np.float32)
    rb1c = np.empty((128, 32), np.float32)
    for e in range(8):
        for k in range(4):
            rb1c[:, 4 * e + k] = rb1[e, 128 * k:128 * k + 128]
    sb1c = np.empty((128, 4), np.float32)
    for k in range(4):
        sb1c[:, k] = sb1[128 * k:128 * k + 128]
    def tile_w(w, kk):
        w = np.asarray(w, np.float32)
        return np.ascontiguousarray(
            w.reshape(kk, 128, -1).transpose(1, 0, 2).reshape(128, -1))

    shared = dict(
        wq=tile_w(L0_attn_wq, 8),
        wk=tile_w(L0_attn_wk, 8),
        wv=tile_w(L0_attn_wv, 8),
        wo=tile_w(L0_attn_wo, 8),
        sw1=tile_w(L0_ffn_sW1, 8),
        sw2=tile_w(L0_ffn_sW2, 4),
        rw1=np.stack([tile_w(np.asarray(L0_ffn_rW1, np.float32)[e], 8)
                      for e in range(8)]),
        rw2=np.stack([tile_w(np.asarray(L0_ffn_rW2, np.float32)[e], 4)
                      for e in range(8)]),
        routingT=tile_w(np.asarray(L0_ffn_routing_W, np.float32).T, 8),
        rb2sb2=np.ascontiguousarray(
            np.concatenate([np.asarray(L0_ffn_rb2, np.float32),
                            np.asarray(L0_ffn_sb2, np.float32)[None, :]], 0)),
        ebias=np.ascontiguousarray(np.asarray(L0_ffn_expert_bias, np.float32)[None, :]),
        rb1c=rb1c, sb1c=sb1c,
        scale_bc=np.ascontiguousarray(np.tile(nsc[None, :], (128, 1))),
    )
    in_maps = []
    for core in range(ncores):
        b, p0 = core // 4, (core % 4) * TPC
        xc = np.zeros((XR, D), np.float32)
        if p0 > 0:
            xc[64:128] = x[b, p0 - 64:p0]
        xc[128:] = x[b, p0:p0 + TPC]
        m = dict(shared)
        m["x"] = xc
        m.update(_host_consts(p0))
        in_maps.append(m)
    return in_maps


def kernel(**inputs):
    in_maps = host_in_maps(**inputs)
    nc = build_program()
    res = run_bass_kernel_spmd(nc, in_maps, core_ids=list(range(8)))
    out = np.empty((2, 16384, D), np.float32)
    for core in range(8):
        b, p0 = core // 4, (core % 4) * TPC
        out[b, p0:p0 + TPC] = res.results[core]["y"]
    return out


# version 24
# speedup vs baseline: 1.0119x; 1.0099x over previous
import contextlib
import numpy as np
import ml_dtypes
import concourse.bass as bass
import concourse.mybir as mybir
from concourse.bass_utils import run_bass_kernel_spmd
from concourse.alu_op_type import AluOpType as OP

F32 = mybir.dt.float32
BF16 = mybir.dt.bfloat16
AF = mybir.ActivationFunctionType

D = 1024
TPC = 4096
XR = TPC + 128
NSTI = 8
NCTX = 640
EPS_LN = 1e-5
EPS_RMS = 1e-6
ROPE_BASE = 10000.0
PIECE = 8192
PHASE = 99
NOMASS = False
NOEVEN = False
DUMP = None
PROF_REGIONS = False
REGIONS = []


class _Stop(Exception):
    pass

_BP = [("wq", 0, 8192), ("wk", 8192, 8192), ("wv", 16384, 8192),
       ("wo", 24576, 8192)]
for _i in range(4):
    _BP.append((f"rw1_{_i}", 36864 + 8192 * _i, 8192))
for _i in range(4):
    _BP.append((f"rw2_{_i}", 69632 + 8192 * _i, 8192))
BLOB_COLS = 69632 + 8192 * 4


def _host_consts(p0):
    half = 32
    theta = 1.0 / ROPE_BASE ** (np.arange(half, dtype=np.float64) / half)
    pos = np.clip(p0 + np.arange(XR) - 128, 0, None).astype(np.float64)
    fr = pos[None, :] * theta[:, None]
    cct = np.tile(np.cos(fr), (4, 1)).astype(ml_dtypes.bfloat16)
    sst = np.tile(np.sin(fr), (4, 1)).astype(ml_dtypes.bfloat16)

    c = np.arange(128)[:, None]
    j = np.arange(64)[None, :]
    band = ((c >= j) & (c <= j + 63)).astype(np.float32)
    m_main = np.concatenate([band, band], axis=1)
    bf = band * (c >= 64) if p0 == 0 else band
    m_first = np.concatenate([bf, bf], axis=1)

    P = np.zeros((128, 128), np.float32)
    for m in range(128):
        g, r = m // 64 * 64, m % 64
        if r < 32:
            P[g + r + 32, m] = -1.0
        else:
            P[g + r - 32, m] = 1.0
    return dict(
        cct=cct, sstb=sst,
        mask_main=np.tile(m_main, (1, 8)).astype(ml_dtypes.bfloat16),
        mask_first=np.tile(m_first, (1, 8)).astype(ml_dtypes.bfloat16),
        prot=P.astype(ml_dtypes.bfloat16),
        eye=np.eye(128, dtype=ml_dtypes.bfloat16),
        onesmat=np.ones((128, 128), dtype=ml_dtypes.bfloat16),
        onesrow=np.ones((1, 128), dtype=ml_dtypes.bfloat16),
        onescol=np.ones((128, 1), dtype=ml_dtypes.bfloat16),
    )


class Sem:
    def __init__(self, stack, nc, name):
        self.h = stack.enter_context(nc.semaphore(name))
        self.n = 0


class Eng:
    def __init__(self, e, sem, step=1):
        self.e = e
        self.sem = sem
        self.step = step
        self.waited = {}

    def wait(self, tok):
        if tok is None:
            return
        sem, v = tok
        if self.waited.get(id(sem), -1) >= v:
            return
        self.e.wait_ge(sem.h, v)
        self.waited[id(sem)] = v

    def fence(self, instr):
        instr.then_inc(self.sem.h, self.step)
        self.sem.n += self.step
        return (self.sem, self.sem.n)


class Buf:
    __slots__ = ("w", "rs", "name", "aliases")

    def __init__(self, name=""):
        self.w = {}
        self.rs = {}
        self.name = name
        self.aliases = []

    @staticmethod
    def _add(d, tok):
        cur = d.get(id(tok[0]))
        if cur is None or tok[1] > cur[1]:
            d[id(tok[0])] = tok

    def add_r(self, tok):
        self._add(self.rs, tok)

    def add_w(self, tok):
        self._add(self.w, tok)


def pre_write(E, bufs):
    for b in bufs:
        for t in b.w.values():
            E.wait(t)
        for t in b.rs.values():
            E.wait(t)
        for a in b.aliases:
            for t in a.w.values():
                E.wait(t)
            for t in a.rs.values():
                E.wait(t)


def pre_read(E, bufs):
    for b in bufs:
        for t in b.w.values():
            E.wait(t)


def selfedge(E, instr):
    E.wait(E.fence(instr))


def post(E, instr, writes=(), reads=()):
    tok = E.fence(instr)
    for b in writes:
        b.w = {id(tok[0]): tok}
        b.rs = {}
    for b in reads:
        b.add_r(tok)
    return tok


class DQueue:

    def __init__(self, e, lanes):
        self.e = e
        self.lanes = lanes
        self.i = 0
        self.waited = {}

    def wait(self, tok):
        if tok is None:
            return
        sem, v = tok
        if self.waited.get(id(sem), -1) >= v:
            return
        self.e.wait_ge(sem.h, v)
        self.waited[id(sem)] = v

    def dma(self, out, in_, writes=(), reads=()):
        ln = self.lanes[self.i % len(self.lanes)]
        self.i += 1
        self.wait((ln, ln.n))
        for b in writes:
            pre_write(self, [b])
        for b in reads:
            pre_read(self, [b])
        instr = self.e.dma_start(out=out, in_=in_)
        instr.then_inc(ln.h, 16)
        ln.n += 16
        tok = (ln, ln.n)
        for b in writes:
            b.w = {id(ln): tok}
            b.rs = {}
        for b in reads:
            b.add_r(tok)
        return tok

    def dma_add(self, out, in_, writes=(), reads=()):
        ln = self.lanes[self.i % len(self.lanes)]
        self.i += 1
        self.wait((ln, ln.n))
        for b in writes:
            pre_write(self, [b])
        for b in reads:
            pre_read(self, [b])
        instr = self.e.dma_start(out=out, in_=in_)
        instr.then_inc(ln.h, 16)
        ln.n += 16
        tok = (ln, ln.n)
        for b in writes:
            b.add_w(tok)
        for b in reads:
            b.add_r(tok)
        return tok


def build_program():
    stack = contextlib.ExitStack()
    nc = bass.Bass("TRN2", target_bir_lowering=False)

    def din(name, shape, dt):
        return nc.dram_tensor(name, list(shape), dt, kind="ExternalInput").ap()

    X = din("x", [XR, D], F32)
    Wq = din("wq", [128, 8192], F32); Wk = din("wk", [128, 8192], F32)
    Wv = din("wv", [128, 8192], F32); Wo = din("wo", [128, 8192], F32)
    SW1 = din("sw1", [128, 4096], F32); SW2 = din("sw2", [128, 4096], F32)
    RW1 = din("rw1", [8, 128, 4096], F32); RW2 = din("rw2", [8, 128, 4096], F32)
    RoutT = din("routingT", [128, 64], F32)
    Rb2Sb2 = din("rb2sb2", [9, D], F32)
    EBias = din("ebias", [1, 8], F32)
    Rb1c = din("rb1c", [128, 32], F32)
    Sb1c = din("sb1c", [128, 4], F32)
    ScaleBC = din("scale_bc", [128, D], F32)
    CCT = din("cct", [128, XR], BF16); SSTd = din("sstb", [128, XR], BF16)
    MaskM = din("mask_main", [128, 1024], BF16)
    MaskF = din("mask_first", [128, 1024], BF16)
    ProtD = din("prot", [128, 128], BF16)
    EyeD = din("eye", [128, 128], BF16)
    OnesMD = din("onesmat", [128, 128], BF16)
    OnesRD = din("onesrow", [1, 128], BF16)
    OnesCD = din("onescol", [128, 1], BF16)
    OUT = nc.dram_tensor("y", [TPC, D], F32, kind="ExternalOutput").ap()
    DBG = nc.dram_tensor("dbg", [128, 8192], F32, kind="ExternalOutput").ap() \
        if DUMP is not None else None
    BLOB = nc.dram_tensor("wblob", [128, BLOB_COLS], BF16, kind="Internal").ap()

    def sbt(name, shape, dt):
        return stack.enter_context(nc.sbuf_tensor(name, list(shape), dt))

    ring = sbt("ring", [128, 2 * PIECE], BF16)
    c_sw2 = sbt("c_sw2", [128, 4096], BF16)
    c_sw1 = sbt("c_sw1", [128, 4096], BF16)
    cc_s = sbt("cc_s", [128, 2 * NCTX], BF16)
    ss_s = sbt("ss_s", [128, 2 * NCTX], BF16)
    x_sb = sbt("x_sb", [128, 5 * D], F32)
    xln = sbt("xln", [128, 2 * D], BF16)
    arn = sbt("arn", [128, 36608], BF16)
    z_sb = sbt("z_sb", [128, 4 * D], F32)
    y1 = sbt("y1", [128, 4 * D], F32)
    trash = sbt("trash", [128, D], F32)
    c_mm = sbt("c_mm", [128, 1024], BF16)
    c_mf = sbt("c_mf", [128, 1024], BF16)
    c_eye = sbt("c_eye", [128, 128], BF16)
    c_ones = sbt("c_ones", [128, 128], BF16)
    c_prot = sbt("c_prot", [128, 128], BF16)
    c_or = sbt("c_or", [1, 128], BF16)
    c_oc = sbt("c_oc", [128, 1], BF16)
    c_scale = sbt("c_scale", [128, D], F32)
    c_rb1 = sbt("c_rb1", [128, 32], F32)
    c_sb1 = sbt("c_sb1", [128, 4], F32)
    c_rt = sbt("c_rt", [128, 64], BF16)
    c_rb2 = sbt("c_rb2", [9, D], BF16)
    c_eb = sbt("c_eb", [1, 8], BF16)
    c_im = sbt("c_im", [128, 8], F32)
    mgA = sbt("mgA", [128, 8], F32)
    mgB = sbt("mgB", [128, 8], F32)
    mgC = sbt("mgC", [128, 8], F32)
    st6 = sbt("st6", [128, 12], F32)
    st2 = sbt("st2", [128, 2], F32)
    stat = sbt("stat", [128, 5 * 4], F32)
    zss = sbt("zss", [128, 4], F32)
    zrs = sbt("zrs", [128, 4], F32)
    rmas = sbt("rmas", [128, 16], F32)
    rm2 = sbt("rm2", [128, 16], F32)
    logi = sbt("logi", [128, 32], F32)
    rsc1 = sbt("rsc1", [128, 32], F32)
    rsc2 = sbt("rsc2", [128, 32], F32)
    rsc3 = sbt("rsc3", [128, 2], F32)
    cf_sb = sbt("cf_sb", [128, 9 * 4], F32)
    cfb = sbt("cfb", [128, 9 * 4], BF16)
    cfT = sbt("cfT", [9, 4 * 128], BF16)

    off = 0
    def _take(n):
        nonlocal off
        o = off; off += n; return o
    A_lnxT = _take(8 * NCTX)
    A_qro = _take(8 * 512)
    A_kro = _take(8 * NCTX)
    A_v = _take(5 * 1024)
    A_v64 = _take(4 * 1024)
    A_qd = _take(2 * 1024)
    A_tau = _take(2 * 1024)
    A_mtk = _take(2 * 1024)
    A_dg = _take(16 * 128)
    A_msgT = _take(8 * 128)
    A_qtp = _take(2 * NCTX)
    A_t1 = _take(2 * NCTX)
    A_t2 = _take(2 * NCTX)
    assert off <= 36608, off
    ATTN_REG = [(["lnxT"], A_lnxT, 5120), (["qro"], A_qro, 4096),
                (["kro"], A_kro, 5120), (["vv"], A_v, 5120),
                (["v64"], A_v64, 4096), (["qd"], A_qd, 2048),
                (["tau0", "tau1"], A_tau, 2048), (["mtk"], A_mtk, 2048),
                (["dg"], A_dg, 2048), (["msgT"], A_msgT, 1024),
                (["qtp0", "qtp1"], A_qtp, 1280), (["t1_0", "t1_1"], A_t1, 1280),
                (["t2_0", "t2_1"], A_t2, 1280)]
    off2 = 0
    def _take2(n):
        nonlocal off2
        o = off2; off2 += n; return o
    F_ln2T = _take2(8 * 512)
    F_sh = _take2(4 * 512)
    F_g = [_take2(4 * 512) for _ in range(8)]
    F_cbc = _take2(512)
    F_dcf = _take2(4 * 128)
    FFN_REG = [(["ln2T"], F_ln2T, 4096), (["sh"], F_sh, 2048)] + \
              [([f"g{e}"], F_g[e], 2048) for e in range(8)] + \
              [(["cbc"], F_cbc, 512), (["dcf"], F_dcf, 512)]
    assert off2 <= off

    bufs = {}
    def B(name):
        if name not in bufs:
            bufs[name] = Buf(name)
        return bufs[name]

    for ans, ao, al in ATTN_REG:
        for fns, fo, fl in FFN_REG:
            if ao < fo + fl and fo < ao + al:
                for an in ans:
                    for fn in fns:
                        B(an).aliases.append(B(fn))
                        B(fn).aliases.append(B(an))

    def ps(name, shape, dt):
        return stack.enter_context(nc.psum_tensor(name, list(shape), dt))
    pA = ps("pA", [128, 512], F32); pB = ps("pB", [128, 512], F32)
    pC = ps("pC", [128, 512], F32); pD = ps("pD", [128, 512], F32)
    pE = ps("pE", [128, 512], F32); pF = ps("pF", [128, 512], F32)
    pTR = ps("pTR", [128, 1024], BF16)
    pM = ps("pM", [128, 512], F32)
    PB = {id(pA): B("pA"), id(pB): B("pB"), id(pC): B("pC"), id(pD): B("pD"),
          id(pE): B("pE"), id(pF): B("pF"), id(pTR): B("pTR"), id(pM): B("pM")}
    def pb(t):
        return PB[id(t)]

    qW = DQueue(nc.sync, [Sem(stack, nc, f"qW{i}") for i in range(4)])
    qX = DQueue(nc.gpsimd, [Sem(stack, nc, f"qX{i}") for i in range(4)])
    qO = DQueue(nc.scalar, [Sem(stack, nc, f"qO{i}") for i in range(2)])
    PE = Eng(nc.tensor, Sem(stack, nc, "sP"))
    V = Eng(nc.vector, Sem(stack, nc, "sV"))
    A = Eng(nc.scalar, Sem(stack, nc, "sA"))
    G = Eng(nc.gpsimd, Sem(stack, nc, "sG"))

    for dst, src, nm in [
            (c_mm[:], MaskM, "c_mm"), (c_mf[:], MaskF, "c_mf"),
            (c_eye[:], EyeD, "c_eye"), (c_ones[:], OnesMD, "c_ones"),
            (c_prot[:], ProtD, "c_prot"), (c_or[:], OnesRD, "c_or"),
            (c_oc[:], OnesCD, "c_oc"), (c_scale[:], ScaleBC, "c_scale"),
            (c_rb1[:], Rb1c, "c_rb1"), (c_sb1[:], Sb1c, "c_sb1"),
            (c_rb2[:], Rb2Sb2, "c_rb2"), (c_eb[:], EBias, "c_eb")]:
        qX.dma(dst, src, writes=[B(nm)])
    qX.dma(c_rt[:], RoutT, writes=[B("c_rt")])
    qX.dma(c_sw2[:], SW2, writes=[B("c_sw2")])
    qX.dma(c_sw1[:], SW1, writes=[B("c_sw1")])

    i = nc.vector.memset(arn[:, A_qd:A_qd + 2048], 0.0)
    post(V, i, writes=[B("qd")])
    i = nc.vector.memset(cf_sb[:], 1.0)
    post(V, i, writes=[B("cf")])
    for _c, _v in enumerate([EPS_LN, 1.0, 1.0 / D, EPS_RMS, -0.5, -1.0, 0.5, 0.125]):
        i = nc.vector.memset(c_im[:, _c:_c + 1], _v)
    post(V, i, writes=[B("c_im")])
    A.wait((V.sem, V.sem.n))

    CAST = {"wq": [(0, Wq)], "wk": [(0, Wk)], "wv": [(0, Wv)], "wo": [(0, Wo)]}
    for _e in range(4):
        CAST[f"rw1_{_e}"] = [(0, RW1[2 * _e]), (4096, RW1[2 * _e + 1])]
        CAST[f"rw2_{_e}"] = [(0, RW2[2 * _e]), (4096, RW2[2 * _e + 1])]
    slotbuf = [B("slot0"), B("slot1")]
    blobbuf = [B(f"blob{pi}") for pi in range(len(_BP))]

    def emit_casts():
        prev = None
        for pi, (nm, boff, ncols) in enumerate(_BP):
            csem = Sem(stack, nc, f"cs{pi}")
            if pi <= 4 and prev is not None:
                nc.gpsimd.wait_ge(prev.h, prev.n)
            for (off_, srcap) in CAST[nm]:
                w = srcap.shape[-1]
                i = nc.gpsimd.dma_start(out=BLOB[:, boff + off_:boff + off_ + w],
                                        in_=srcap)
                i.then_inc(csem.h, 16)
                csem.n += 16
            blobbuf[pi].w = {id(csem): (csem, csem.n)}
            prev = csem

    def mm(out_ap, lhsT, rhs, start, stop):
        return nc.tensor.matmul(out_ap, lhsT, rhs, start=start, stop=stop)

    def _reg(tag):
        if PROF_REGIONS:
            REGIONS.append((tag, int(nc.get_next_instruction_name().split("-")[1])))

    from concourse.hw_specs import get_activation_tables
    _tabs = list(get_activation_tables(nc.m.arch).values())

    def _set_id(funcs):
        for _i, _fs in enumerate(_tabs):
            if all(f in _fs for f in funcs):
                return _i
        raise ValueError(funcs)

    SET_IDS = {
        "sigmoid": _set_id([AF.Sigmoid, AF.Square, AF.Copy]),
        "nlexp": _set_id([AF.Ln, AF.Exp, AF.Abs, AF.Relu, AF.Square, AF.Copy]),
        "silu": _set_id([AF.Silu, AF.Square, AF.Copy]),
    }
    cur_set = [None]

    def use_set(name):
        if cur_set[0] != name:
            nc.scalar.add_instruction(mybir.InstLoadActFuncSet(
                name=nc.get_next_instruction_name(),
                act_func_set_id=SET_IDS[name], ins=[], outs=[]))
            cur_set[0] = name

    lseq = [0]
    def load_piece(pi):
        sl = lseq[0] % 2
        lseq[0] += 1
        sbf = slotbuf[sl]
        nm, boff, ncols = _BP[pi]
        qW.dma(ring[:, sl * PIECE:sl * PIECE + ncols], BLOB[:, boff:boff + ncols],
               writes=[sbf], reads=[blobbuf[pi]])
        return sl * PIECE, sbf

    MAGIC = 0x5f3759df

    def rsqrt_magic(dst, pre, n):
        y = mgB[:, 0:n]
        h = mgC[:, 0:n]
        pre_read(V, [B("mg")])
        selfedge(V, nc.vector.tensor_scalar(y.bitcast(mybir.dt.int32),
                                            pre.bitcast(mybir.dt.int32), 1, 0,
                                            OP.logical_shift_right,
                                            OP.logical_shift_right))
        selfedge(V, nc.vector.tensor_scalar(y.bitcast(mybir.dt.int32),
                                            y.bitcast(mybir.dt.int32), -1, MAGIC,
                                            OP.mult, OP.add))
        for _ in range(2):
            selfedge(V, nc.vector.tensor_tensor(h, y, y, OP.mult))
            selfedge(V, nc.vector.tensor_tensor(h, h, pre, OP.mult))
            selfedge(V, nc.vector.tensor_scalar(h, h, -0.5, 1.5, OP.mult, OP.add))
            selfedge(V, nc.vector.tensor_tensor(y, y, h, OP.mult))
        i = nc.vector.tensor_copy(dst, y)
        return i

    def ckpt(n):
        if PHASE <= n:
            raise _Stop()

    def tap(tag, ap, bufs, cols):
        if DUMP != tag:
            return
        qX.dma(DBG[:, 0:cols], ap, reads=bufs)
        raise _Stop()

    try:
        def prologue_loads(s):
            r0 = 512 * s
            sso = (s % 2) * NCTX
            xbufs = [B(f"x{ct}") for ct in range(5)]
            for ct in range(5):
                qX.dma(x_sb[:, ct * D:(ct + 1) * D],
                       X[r0 + 128 * ct:r0 + 128 * ct + 128, :], writes=[xbufs[ct]])
            ccb = B("cc")
            qX.dma(cc_s[:, sso:sso + NCTX], CCT[:, r0:r0 + NCTX], writes=[ccb])
            qX.dma_add(ss_s[:, sso:sso + NCTX], SSTd[:, r0:r0 + NCTX], writes=[ccb])
            return xbufs, ccb

        def prologue(s, pre=None):
            if pre is None:
                pre = prologue_loads(s)
            xbufs, ccb = pre
            lnxTb = B("lnxT")
            for ct in range(5):
                xb = xbufs[ct]
                pre_read(V, [xb])
                pre_write(V, [B("st6"), B("st2"), B("stat"), B("mg")])
                nc.vector.bn_stats(st6[:, 0:6], x_sb[:, ct * D:ct * D + 512])
                selfedge(V, nc.vector.bn_stats(st6[:, 6:12],
                                               x_sb[:, ct * D + 512:ct * D + 1024]))
                selfedge(V, nc.vector.bn_aggr(st2[:], st6[:]))
                nc.vector.tensor_copy(stat[:, 4 * ct:4 * ct + 1], st2[:, 0:1])
                i = nc.vector.tensor_scalar(mgA[:, ct:ct + 1], st2[:, 1:2],
                                            1.0, EPS_LN, OP.mult, OP.add)
                post(V, i, writes=[B("st6"), B("st2"), B("stat"), B("mg")], reads=[xb])
            pre_write(V, [B("stat"), B("mg")])
            selfedge(V, rsqrt_magic(
                stat[:].rearrange("p (c f) -> p c f", f=4)[:, :, 1:2], mgA[:, 0:5], 5))
            selfedge(V, nc.vector.tensor_tensor(
                stat[:].rearrange("p (c f) -> p c f", f=4)[:, :, 2:3],
                stat[:].rearrange("p (c f) -> p c f", f=4)[:, :, 0:1],
                stat[:].rearrange("p (c f) -> p c f", f=4)[:, :, 1:2], OP.mult))
            i = nc.vector.tensor_scalar(
                stat[:].rearrange("p (c f) -> p c f", f=4)[:, :, 2:3],
                stat[:].rearrange("p (c f) -> p c f", f=4)[:, :, 2:3],
                -1.0, 0.0, OP.mult, OP.add)
            post(V, i, writes=[B("stat"), B("mg")])
            xslot = [(xln, 0, B("xln0")), (xln, D, B("xln1")),
                     (arn, A_tau, B("tau0")), (arn, A_tau + D, B("tau1")),
                     (arn, A_mtk, B("mtk"))]
            for ct in range(5):
                t, xo, xlb = xslot[ct]
                pre_read(V, [xbufs[ct], B("stat")])
                pre_write(V, [xlb])
                i = nc.vector.tensor_scalar(t[:, xo:xo + D], x_sb[:, ct * D:(ct + 1) * D],
                                            stat[:, 4 * ct + 1:4 * ct + 2],
                                            stat[:, 4 * ct + 2:4 * ct + 3],
                                            OP.mult, OP.add)
                post(V, i, writes=[xlb], reads=[xbufs[ct], B("stat")])
            for ct in range(5):
                t, xo, xlb = xslot[ct]
                pre_read(PE, [xlb, B("c_eye")])
                pre_write(PE, [pb(pTR)])
                for ch in range(8):
                    i = nc.tensor.transpose(pTR[:, 128 * ch:128 * ch + 128],
                                            t[:, xo + 128 * ch:xo + 128 * ch + 128],
                                            c_eye[:])
                post(PE, i, writes=[pb(pTR)], reads=[xlb, B("c_eye")])
                pre_read(V, [pb(pTR)])
                pre_write(V, [lnxTb])
                i = nc.vector.tensor_copy(
                    arn[:, A_lnxT:A_lnxT + 8 * NCTX].rearrange(
                        "p (k c) -> p k c", k=8)[:, :, 128 * ct:128 * ct + 128],
                    pTR[:].rearrange("p (k c) -> p k c", k=8))
                post(V, i, writes=[lnxTb], reads=[pb(pTR)])
            return xbufs, ccb, lnxTb

        _reg("init")
        _pre0 = prologue_loads(0)
        emit_casts()
        STV = {}
        STV[0] = prologue(0, _pre0)
        for s in range(NSTI):
            r0 = 512 * s
            sso = (s % 2) * NCTX

            _reg(f"s{s}:led")
            wqo, wqb = load_piece(0)
            wko, wkb = load_piece(1)
            xbufs, ccb, lnxTb = STV[s]

            tap("lnxT", arn[:, A_lnxT:A_lnxT + 5120], [lnxTb], 5120)
            tap("stat", stat[:].rearrange("p (c f) -> p c f", f=4)[:, :, 1:3], [B("stat")], 10)
            ckpt(2)
            qrob, krob, vb = B("qro"), B("kro"), B("vv")

            def stageA(ot):
                B0 = 8 * s + 2 * ot
                qdb = B("qd")
                for blk in range(2):
                    Bg = B0 + blk
                    qdo = A_qd + 1024 * blk
                    toko = 64 * (Bg - 8 * s - 2)
                    qW.dma_add(
                        arn[0:64, qdo:qdo + 1024].rearrange(
                            "p (g c) -> p g c", g=8)[:, :, 0:64],
                        arn[0:64, A_qro:A_qro + 4096].rearrange(
                            "p (g c) -> p g c", g=8)[:, :, toko:toko + 64],
                        writes=[qdb], reads=[qrob])
                    qW.dma_add(
                        arn[64:128, qdo:qdo + 1024].rearrange(
                            "p (g c) -> p g c", g=8)[:, :, 64:128],
                        arn[64:128, A_qro:A_qro + 4096].rearrange(
                            "p (g c) -> p g c", g=8)[:, :, toko:toko + 64],
                        writes=[qdb], reads=[qrob])

            def stageB_mms(ot, blk):
                Bg = 8 * s + 2 * ot + blk
                c0 = 64 * (Bg - 1 - 8 * s)
                pl, ph = (pC, pD) if blk == 0 else (pE, pF)
                qdo = A_qd + 1024 * blk
                pre_read(PE, [krob, B("qd")])
                pre_write(PE, [pb(pl), pb(ph)])
                for p in range(8):
                    rh = arn[:, qdo + 128 * p:qdo + 128 * p + 128]
                    pt = pl if p < 4 else ph
                    oc = 128 * (p % 4)
                    lt = arn[:, A_kro + p * NCTX + c0:A_kro + p * NCTX + c0 + 128]
                    i = mm(pt[:, oc:oc + 128], lt, rh, True, True)
                post(PE, i, writes=[pb(pl), pb(ph)], reads=[krob, B("qd")])

            def stageB_act(ot, blk):
                Bg = 8 * s + 2 * ot + blk
                pl, ph = (pC, pD) if blk == 0 else (pE, pF)
                tauo = A_tau + 1024 * blk
                taub = B(f"tau{blk}")
                pre_read(A, [pb(pl), pb(ph)])
                pre_write(A, [taub])
                nc.scalar.activation(arn[:, tauo:tauo + 512], pl[:],
                                     AF.Sigmoid, scale=c_im[:, 7:8])
                i = nc.scalar.activation(arn[:, tauo + 512:tauo + 1024], ph[:],
                                         AF.Sigmoid, scale=c_im[:, 7:8])
                post(A, i, writes=[taub], reads=[pb(pl), pb(ph)])
                mk, mkb = (c_mf, B("c_mf")) if Bg == 2 else (c_mm, B("c_mm"))
                pre_read(G, [taub, mkb])
                pre_write(G, [taub])
                i = nc.gpsimd.tensor_tensor(arn[:, tauo:tauo + 1024],
                                            arn[:, tauo:tauo + 1024], mk[:],
                                            OP.mult)
                post(G, i, writes=[taub], reads=[mkb])

            def stageB(ot, blks=(0, 1)):
                for blk in blks:
                    stageB_mms(ot, blk)
                    stageB_act(ot, blk)

            def proj_rope(kind, wo_, wb_, dsto, dstb, fillers=()):
                ncols = 512 if kind == "q" else 640
                base = 128 if kind == "q" else 0

                qtp_offs = (A_qtp, A_qtp + NCTX, A_dg)

                def stage1(ch):
                    acc = pA if ch % 2 == 0 else pC
                    tail = ncols == 640
                    pre_read(PE, [lnxTb, wb_])
                    pre_write(PE, [pb(acc)] + ([pb(pM)] if tail else []))
                    for k in range(8):
                        lt = ring[:, wo_ + k * 1024 + 128 * ch:wo_ + k * 1024 + 128 * ch + 128]
                        rh = arn[:, A_lnxT + k * NCTX + base:A_lnxT + k * NCTX + base + 512]
                        i = mm(acc[:], lt, rh, k == 0, k == 7)
                    if tail:
                        for k in range(8):
                            lt = ring[:, wo_ + k * 1024 + 128 * ch:wo_ + k * 1024 + 128 * ch + 128]
                            rh = arn[:, A_lnxT + k * NCTX + 512:A_lnxT + k * NCTX + 640]
                            i = mm(pM[:, 0:128] if ch % 2 == 0 else pM[:, 256:384],
                                   lt, rh, k == 0, k == 7)
                    post(PE, i, writes=[pb(acc)] + ([pb(pM)] if tail else []),
                         reads=[lnxTb, wb_])
                    qtp_o = qtp_offs[ch % 3]
                    qtb = B(f"qtp{ch % 3}")
                    pre_read(A, [pb(acc)] + ([pb(pM)] if tail else []))
                    pre_write(A, [qtb])
                    i = nc.scalar.copy(arn[:, qtp_o:qtp_o + 512], acc[:])
                    if tail:
                        i = nc.scalar.copy(arn[:, qtp_o + 512:qtp_o + 640],
                                           pM[:, 0:128] if ch % 2 == 0 else pM[:, 256:384])
                    post(A, i, writes=[qtb],
                         reads=[pb(acc)] + ([pb(pM)] if tail else []))

                def stage2(ch):
                    tail = ncols == 640
                    rot = pB if ch % 2 == 0 else pD
                    qtp_o = qtp_offs[ch % 3]
                    t1o = A_t1 + (ch % 2) * NCTX
                    t2o = A_t2 + (ch % 2) * NCTX
                    qtb = B(f"qtp{ch % 3}")
                    t1b = B(f"t1_{ch % 2}")
                    t2b = B(f"t2_{ch % 2}")
                    pre_read(PE, [qtb, B("c_prot")])
                    pre_write(PE, [pb(rot)] + ([pb(pM)] if tail else []))
                    i = mm(rot[:], c_prot[:], arn[:, qtp_o:qtp_o + 512], True, True)
                    if tail:
                        i = mm(pM[:, 128:256] if ch % 2 == 0 else pM[:, 384:512],
                               c_prot[:], arn[:, qtp_o + 512:qtp_o + 640], True, True)
                    post(PE, i, writes=[pb(rot)] + ([pb(pM)] if tail else []),
                         reads=[qtb, B("c_prot")])
                    pre_read(V, [qtb, ccb])
                    pre_write(V, [t1b])
                    i = nc.vector.tensor_tensor(
                        arn[:, t1o:t1o + ncols], arn[:, qtp_o:qtp_o + ncols],
                        cc_s[:, sso + base:sso + base + ncols], OP.mult)
                    post(V, i, writes=[t1b], reads=[qtb, ccb])
                    pre_read(V, [pb(rot), ccb] + ([pb(pM)] if tail else []))
                    pre_write(V, [t2b])
                    i = nc.vector.tensor_tensor(arn[:, t2o:t2o + 512], rot[:],
                                                ss_s[:, sso + base:sso + base + 512], OP.mult)
                    if tail:
                        i = nc.vector.tensor_tensor(
                            arn[:, t2o + 512:t2o + 640],
                            pM[:, 128:256] if ch % 2 == 0 else pM[:, 384:512],
                            ss_s[:, sso + base + 512:sso + base + 640], OP.mult)
                    post(V, i, writes=[t2b],
                         reads=[pb(rot), ccb] + ([pb(pM)] if tail else []))
                    pre_read(V, [t1b, t2b])
                    pre_write(V, [dstb])
                    i = nc.vector.tensor_tensor(
                        arn[:, dsto + ch * ncols:dsto + (ch + 1) * ncols],
                        arn[:, t1o:t1o + ncols], arn[:, t2o:t2o + ncols], OP.add)
                    post(V, i, writes=[dstb], reads=[t1b, t2b])

                fill = list(fillers)
                for ch in range(8):
                    stage1(ch)
                    if ch >= 2:
                        stage2(ch - 2)
                        if fill:
                            fill.pop(0)()
                stage2(6)
                stage2(7)
                while fill:
                    fill.pop(0)()

            _reg(f"s{s}:qkv_q")
            proj_rope("q", wqo, wqb, A_qro, qrob)
            wvo, wvb = load_piece(2)
            stageA(1)
            v64b = B("v64")

            def vtc(tc):
                for bk in range(2):
                    pt = pE if bk == 0 else pF
                    pre_read(PE, [lnxTb, wvb])
                    pre_write(PE, [pb(pt)])
                    for k in range(8):
                        lt = arn[:, A_lnxT + k * NCTX + 128 * tc:A_lnxT + k * NCTX + 128 * tc + 128]
                        rh = ring[:, wvo + k * 1024 + 512 * bk:wvo + k * 1024 + 512 * bk + 512]
                        i = mm(pt[:], lt, rh, k == 0, k == 7)
                    post(PE, i, writes=[pb(pt)], reads=[lnxTb, wvb])
                    if bk == 0:
                        pre_read(A, [pb(pt)])
                        pre_write(A, [vb])
                        i = nc.scalar.activation(
                            arn[:, A_v + 1024 * tc:A_v + 1024 * tc + 512],
                            pt[:], AF.Copy)
                        post(A, i, writes=[vb], reads=[pb(pt)])
                    else:
                        pre_read(V, [pb(pt)])
                        pre_write(V, [vb])
                        i = nc.vector.tensor_copy(
                            arn[:, A_v + 1024 * tc + 512:A_v + 1024 * tc + 1024],
                            pt[:])
                        post(V, i, writes=[vb], reads=[pb(pt)])
                if tc >= 1:
                    j = tc - 1
                    if j == 0:
                        qX.dma(arn[0:64, A_v64 + 1024 * j:A_v64 + 1024 * j + 1024],
                               arn[64:128, A_v + 1024 * j:A_v + 1024 * j + 1024],
                               writes=[v64b], reads=[vb])
                    else:
                        qX.dma_add(arn[0:64, A_v64 + 1024 * j:A_v64 + 1024 * j + 1024],
                                   arn[64:128, A_v + 1024 * j:A_v + 1024 * j + 1024],
                                   writes=[v64b], reads=[vb])
                    qX.dma_add(arn[64:128, A_v64 + 1024 * j:A_v64 + 1024 * j + 1024],
                               arn[0:64, A_v + 1024 * tc:A_v + 1024 * tc + 1024],
                               writes=[v64b], reads=[vb])

            _reg(f"s{s}:qkv_k")
            proj_rope("k", wko, wkb, A_kro, krob,
                      fillers=[lambda t=t: vtc(t) for t in range(5)])
            tap("qro", arn[:, A_qro:A_qro + 4096], [qrob], 4096)
            tap("kro", arn[:, A_kro:A_kro + 5120], [krob], 5120)
            ckpt(3)
            woo, wob = load_piece(3)
            use_set("sigmoid")
            stageB_mms(1, 0)
            rw1h = [None] * 4
            rw1h[0] = load_piece(4)
            _reg(f"s{s}:qkv_v")
            stageB_act(1, 0)
            stageB_mms(1, 1)
            stageB_act(1, 1)

            ckpt(4)
            tap("vv", arn[:, A_v:A_v + 5120], [vb], 5120)

            tap("v64", arn[:, A_v64:A_v64 + 4096], [v64b], 4096)

            ln2Tb = B("ln2T")

            def epilogue_pre(ot):
                tc = ot - 1
                y1b = B(f"y1_{ot}")
                zb = B(f"z{ot}")
                pre_read(V, [B("zss")])
                pre_write(V, [B("mg"), B("zrs")])
                selfedge(V, nc.vector.tensor_scalar(mgA[:, 6:7], zss[:, tc:tc + 1],
                                                    1.0 / D, EPS_RMS, OP.mult, OP.add))
                i = rsqrt_magic(zrs[:, tc:tc + 1], mgA[:, 6:7], 1)
                post(V, i, writes=[B("mg"), B("zrs")], reads=[B("zss")])
                pre_read(V, [zb, B("zrs"), B("c_scale")])
                pre_write(V, [zb])
                i = nc.vector.scalar_tensor_tensor(
                    z_sb[:, tc * D:ot * D], z_sb[:, tc * D:ot * D],
                    zrs[:, tc:tc + 1], c_scale[:], OP.mult, OP.mult)
                post(V, i, writes=[zb], reads=[B("zrs"), B("c_scale")])
                pre_read(G, [zb, xbufs[ot]])
                pre_write(G, [y1b])
                i = nc.gpsimd.tensor_tensor(
                    y1[:, tc * D:ot * D], z_sb[:, tc * D:ot * D],
                    x_sb[:, ot * D:(ot + 1) * D], OP.add)
                post(G, i, writes=[y1b], reads=[zb, xbufs[ot]])
                pre_read(V, [y1b])
                pre_write(V, [B("st6"), B("st2"), B("stat"), B("mg")])
                nc.vector.bn_stats(st6[:, 0:6], y1[:, tc * D:tc * D + 512])
                selfedge(V, nc.vector.bn_stats(st6[:, 6:12],
                                               y1[:, tc * D + 512:tc * D + 1024]))
                selfedge(V, nc.vector.bn_aggr(st2[:], st6[:]))
                selfedge(V, nc.vector.tensor_scalar(mgA[:, 7:8], st2[:, 1:2],
                                                    1.0, EPS_LN, OP.mult, OP.add))
                selfedge(V, rsqrt_magic(stat[:, 4 * tc + 1:4 * tc + 2], mgA[:, 7:8], 1))
                i = nc.vector.tensor_scalar(stat[:, 4 * tc + 2:4 * tc + 3], st2[:, 0:1],
                                            stat[:, 4 * tc + 1:4 * tc + 2], -1.0,
                                            OP.mult, OP.mult)
                post(V, i, writes=[B("st6"), B("st2"), B("stat"), B("mg")], reads=[y1b])
                xlb = B(f"xln{tc % 2}")
                xo = (tc % 2) * D
                pre_read(V, [y1b, B("stat")])
                pre_write(V, [xlb])
                i = nc.vector.tensor_scalar(xln[:, xo:xo + D], y1[:, tc * D:(tc + 1) * D],
                                            stat[:, 4 * tc + 1:4 * tc + 2],
                                            stat[:, 4 * tc + 2:4 * tc + 3],
                                            OP.mult, OP.add)
                post(V, i, writes=[xlb], reads=[y1b, B("stat")])

            def epilogue_post(ot):
                tc = ot - 1
                xlb = B(f"xln{tc % 2}")
                xo = (tc % 2) * D
                pre_read(PE, [xlb, B("c_eye")])
                pre_write(PE, [pb(pTR)])
                for ch in range(8):
                    i = nc.tensor.transpose(pTR[:, 128 * ch:128 * ch + 128],
                                            xln[:, xo + 128 * ch:xo + 128 * ch + 128],
                                            c_eye[:])
                post(PE, i, writes=[pb(pTR)], reads=[xlb, B("c_eye")])
                hb = B("ln2Th0") if tc <= 1 else B("ln2Th1")
                pre_read(V, [pb(pTR)])
                pre_write(V, [ln2Tb, hb])
                i = nc.vector.tensor_copy(
                    arn[:, F_ln2T:F_ln2T + 4096].rearrange(
                        "p (k c) -> p k c", k=8)[:, :, 128 * tc:128 * tc + 128],
                    pTR[:].rearrange("p (k c) -> p k c", k=8))
                post(V, i, writes=[ln2Tb, hb], reads=[pb(pTR)])

            def routing_mm(tc):
                pre_read(PE, [ln2Tb, B("c_rt"), B("c_or"), B("c_eb")])
                pre_write(PE, [pb(pM)])
                for k in range(8):
                    lt = arn[:, F_ln2T + k * 512 + 128 * tc:F_ln2T + k * 512 + 128 * tc + 128]
                    mm(pM[:, 384:392], lt, c_rt[:, 8 * k:8 * k + 8], k == 0, False)
                i = mm(pM[:, 384:392], c_or[:], c_eb[:], False, True)
                post(PE, i, writes=[pb(pM)],
                     reads=[ln2Tb, B("c_rt"), B("c_or"), B("c_eb")])
                pre_read(A, [pb(pM)])
                pre_write(A, [B("logi")])
                i = nc.scalar.activation(logi[:, 8 * tc:8 * tc + 8], pM[:, 384:392],
                                         AF.Copy)
                post(A, i, writes=[B("logi")], reads=[pb(pM)])

            def top2_pair(tcs):
                use_set("nlexp")
                n = 8 * len(tcs)
                o = 8 * tcs[0]
                pre_read(A, [B("logi")])
                pre_write(A, [B("rsc1"), B("rsc2")])
                selfedge(A, nc.scalar.activation(rsc1[:, o:o + n],
                                                 logi[:, o:o + n], AF.Abs))
                selfedge(A, nc.scalar.activation(rsc1[:, o:o + n], rsc1[:, o:o + n],
                                                 AF.Exp, scale=c_im[:, 5:6]))
                selfedge(A, nc.scalar.activation(rsc1[:, o:o + n], rsc1[:, o:o + n],
                                                 AF.Ln, bias=c_im[:, 1:2]))
                i = nc.scalar.activation(rsc2[:, o:o + n], logi[:, o:o + n], AF.Relu)
                post(A, i, writes=[B("rsc1"), B("rsc2")], reads=[B("logi")])
                pre_read(V, [B("rsc1"), B("rsc2")])
                pre_write(V, [B("rsc1")])
                i = nc.vector.tensor_tensor(rsc1[:, o:o + n], rsc1[:, o:o + n],
                                            rsc2[:, o:o + n], OP.add)
                post(V, i, writes=[B("rsc1")], reads=[B("rsc2")])
                pre_read(A, [B("rsc1")])
                pre_write(A, [B("rsc2")])
                selfedge(A, nc.scalar.activation(rsc2[:, o:o + n], rsc1[:, o:o + n],
                                                 AF.Ln))
                i = nc.scalar.activation(rsc2[:, o:o + n], rsc2[:, o:o + n],
                                         AF.Exp, scale=c_im[:, 6:7])
                post(A, i, writes=[B("rsc2")], reads=[B("rsc1")])
                for tc in tcs:
                    c = 8 * tc
                    pre_read(V, [B("rsc2")])
                    pre_write(V, [B("rsc3"), B("rsc1"), B("cf")])
                    selfedge(V, nc.vector.reduce_max(rsc3[:, 0:1], rsc2[:, c:c + 8],
                                                     mybir.AxisListType.X))
                    selfedge(V, nc.vector.tensor_scalar(
                        rsc1[:, c:c + 8], rsc2[:, c:c + 8], rsc3[:, 0:1],
                        -1e9, OP.is_equal, OP.mult))
                    selfedge(V, nc.vector.tensor_tensor(
                        rsc1[:, c:c + 8], rsc1[:, c:c + 8], rsc2[:, c:c + 8], OP.add))
                    selfedge(V, nc.vector.reduce_max(rsc3[:, 1:2], rsc1[:, c:c + 8],
                                                     mybir.AxisListType.X))
                    selfedge(V, nc.vector.tensor_scalar(
                        rsc1[:, c:c + 8], rsc2[:, c:c + 8], rsc3[:, 1:2],
                        0.0, OP.is_ge, OP.add))
                    i = nc.vector.tensor_tensor(cf_sb[:, 9 * tc:9 * tc + 8],
                                                rsc2[:, c:c + 8], rsc1[:, c:c + 8],
                                                OP.mult)
                    post(V, i, writes=[B("rsc3"), B("rsc1"), B("cf")],
                         reads=[B("rsc2")])

            def w1_half(src, src_off, srcb, dst_off, dstb, bias_ap, biasb, h,
                        banks, ltok):
                for m in range(4):
                    pt = banks[m]
                    pre_read(PE, [srcb, ltok, biasb])
                    pre_write(PE, [pb(pt)])
                    for k in range(8):
                        lt = src[:, src_off + k * 512 + 128 * m:src_off + k * 512 + 128 * m + 128]
                        rh = arn[:, F_ln2T + k * 512 + 256 * h:F_ln2T + k * 512 + 256 * h + 256]
                        i = mm(pt[:, 0:256], lt, rh, k == 0, k == 7)
                    post(PE, i, writes=[pb(pt)], reads=[srcb, ltok, biasb])
                    pre_read(A, [pb(pt), biasb])
                    pre_write(A, [dstb])
                    i = nc.scalar.activation(
                        arn[:, dst_off + 512 * m + 256 * h:dst_off + 512 * m + 256 * h + 256],
                        pt[:, 0:256], AF.Silu, bias=bias_ap[:, m:m + 1])
                    post(A, i, writes=[dstb], reads=[pb(pt), biasb])



            def stageC(ot, blk):
                B0 = 8 * s + 2 * ot
                Bg = B0 + blk
                mbase = 256 + 16 * (ot % 2)
                mtkb = B("mtk")
                msgTb = B("msgT")
                taub = B(f"tau{blk}")
                tauo = A_tau + 1024 * blk
                pl, ph = (pC, pD) if blk == 0 else (pE, pF)
                pre_read(PE, [taub, vb, B("v64"), B("c_oc")])
                pre_write(PE, [pb(pl), pb(ph), pb(pM)])
                for p in range(8):
                    pt = pl if p < 4 else ph
                    oc = 128 * (p % 4)
                    mc = mbase + 8 * blk + p
                    lt = arn[:, tauo + 128 * p:tauo + 128 * p + 128]
                    if Bg % 2 == 1:
                        vt = (Bg - 1) // 2 - 4 * s
                        rh = arn[:, A_v + 1024 * vt + 128 * p:A_v + 1024 * vt + 128 * p + 128]
                    else:
                        j = (Bg - 8 * s - 2) // 2
                        rh = arn[:, A_v64 + 1024 * j + 128 * p:A_v64 + 1024 * j + 128 * p + 128]
                    i = mm(pt[:, oc:oc + 128], lt, rh, True, True)
                    i = mm(pM[:, mc:mc + 1], lt, c_oc[:], True, True)
                post(PE, i, writes=[pb(pl), pb(ph), pb(pM)],
                     reads=[taub, vb, B("v64"), B("c_oc")])
                mo = A_mtk + 1024 * blk
                pre_read(V, [pb(pM)])
                pre_write(V, [B("rmas"), B("rm2")])
                selfedge(V, nc.vector.tensor_scalar(
                    rmas[:, 8 * blk:8 * blk + 8], pM[:, mbase + 8 * blk:mbase + 8 * blk + 8],
                    1e-6, 0.0, OP.max, OP.add))
                i = nc.vector.reciprocal(rm2[:, 8 * blk:8 * blk + 8],
                                         rmas[:, 8 * blk:8 * blk + 8])
                post(V, i, writes=[B("rmas"), B("rm2")], reads=[pb(pM)])
                pre_read(A, [pb(pl), B("rm2")])
                pre_write(A, [mtkb])
                for p in range(4):
                    idx = 8 * blk + p
                    i = nc.scalar.activation(
                        arn[:, mo + 128 * p:mo + 128 * p + 128],
                        pl[:, 128 * p:128 * p + 128], AF.Copy,
                        scale=rm2[:, idx:idx + 1])
                post(A, i, writes=[mtkb], reads=[pb(pl), B("rm2")])
                pre_read(V, [pb(ph), B("rm2")])
                pre_write(V, [mtkb])
                for p in range(4, 8):
                    idx = 8 * blk + p
                    i = nc.vector.tensor_scalar(
                        arn[:, mo + 128 * p:mo + 128 * p + 128],
                        ph[:, 128 * (p % 4):128 * (p % 4) + 128],
                        rm2[:, idx:idx + 1], 0.0, OP.mult, OP.add)
                post(V, i, writes=[mtkb], reads=[pb(ph), B("rm2")])
                pre_read(PE, [mtkb, B("c_eye")])
                pre_write(PE, [pb(pl), pb(ph)])
                for p in range(8):
                    pt = pl if p < 4 else ph
                    oc = 128 * (p % 4)
                    i = mm(pt[:, oc:oc + 128],
                           arn[:, mo + 128 * p:mo + 128 * p + 128],
                           c_eye[:], True, True)
                post(PE, i, writes=[pb(pl), pb(ph)], reads=[mtkb, B("c_eye")])
                pre_read(A, [pb(pl), pb(ph)])
                pre_write(A, [msgTb])
                for par, (q0, q1) in enumerate(((0, 64), (64, 128))):
                    nc.scalar.copy(
                        arn[q0:q1, A_msgT:A_msgT + 512].rearrange(
                            "p (g c) -> p g c", g=4)[:, :, 64 * blk:64 * blk + 64],
                        pl[q0:q1, :].rearrange(
                            "p (g c) -> p g c", g=4)[:, :, q0:q0 + 64])
                    i = nc.scalar.copy(
                        arn[q0:q1, A_msgT + 512:A_msgT + 1024].rearrange(
                            "p (g c) -> p g c", g=4)[:, :, 64 * blk:64 * blk + 64],
                        ph[q0:q1, :].rearrange(
                            "p (g c) -> p g c", g=4)[:, :, q0:q0 + 64])
                post(A, i, writes=[msgTb], reads=[pb(pl), pb(ph)])

            def stageW(ot):
                msgTb = B("msgT")
                pre_read(PE, [msgTb, wob])
                pre_write(PE, [pb(pA), pb(pB)])
                for bk in range(2):
                    pt = pA if bk == 0 else pB
                    for ch in range(8):
                        lt = arn[:, A_msgT + 128 * ch:A_msgT + 128 * ch + 128]
                        rh = ring[:, woo + ch * 1024 + 512 * bk:woo + ch * 1024 + 512 * bk + 512]
                        i = mm(pt[:], lt, rh, ch == 0, ch == 7)
                post(PE, i, writes=[pb(pA), pb(pB)], reads=[msgTb, wob])

            def stageZ1A(ot):
                zb = B(f"z{ot}")
                pre_read(A, [pb(pA), pb(pB)])
                pre_write(A, [zb])
                nc.scalar.copy(z_sb[:, (ot - 1) * D:(ot - 1) * D + 512], pA[:])
                i = nc.scalar.copy(z_sb[:, (ot - 1) * D + 512:ot * D], pB[:])
                post(A, i, writes=[zb], reads=[pb(pA), pb(pB)])

            def stageZ1V(ot):
                ct = ot
                zb = B(f"z{ot}")
                pre_read(V, [zb, xbufs[ct], B("stat")])
                pre_write(V, [zb])
                selfedge(V, nc.vector.scalar_tensor_tensor(
                    z_sb[:, (ot - 1) * D:ot * D],
                    x_sb[:, ct * D:(ct + 1) * D],
                    stat[:, 4 * ct + 1:4 * ct + 2],
                    z_sb[:, (ot - 1) * D:ot * D], OP.mult, OP.add))
                i = nc.vector.tensor_scalar(
                    z_sb[:, (ot - 1) * D:ot * D], z_sb[:, (ot - 1) * D:ot * D],
                    stat[:, 4 * ct + 2:4 * ct + 3], 0.0, OP.add, OP.add)
                post(V, i, writes=[zb], reads=[xbufs[ct], B("stat")])

            def stageZ2(ot):
                zb = B(f"z{ot}")
                pre_read(A, [zb])
                pre_write(A, [B("zss"), B("trash")])
                i = nc.scalar.activation(trash[:], z_sb[:, (ot - 1) * D:ot * D],
                                         AF.Square, accum_out=zss[:, ot - 1:ot])
                post(A, i, writes=[B("zss"), B("trash")], reads=[zb])

            _reg(f"s{s}:attn")
            for ot in range(1, 5):
                _reg(f"s{s}:at{ot}")
                stageC(ot, 0)
                if ot >= 2:
                    stageZ1V(ot - 1)
                    stageZ2(ot - 1)
                if ot >= 3:
                    epilogue_post(ot - 2)
                stageC(ot, 1)
                if ot < 4:
                    stageA(ot + 1)
                    stageB(ot + 1)
                else:
                    use_set("silu")
                    routing_mm(0)
                    routing_mm(1)
                stageW(ot)
                if ot >= 2:
                    epilogue_pre(ot - 1)
                if ot == 4:
                    rw1h[1] = load_piece(5)
                    w1_half(c_sw1, 0, B("c_sw1"), F_sh, B("sh"), c_sb1,
                            B("c_sb1"), 0, (pC, pD, pE, pF), B("ln2Th0"))
                    w1_half(ring, rw1h[0][0], rw1h[0][1], F_g[0], B("g0"),
                            c_rb1[:, 0:4], B("c_rb1"), 0, (pC, pD, pE, pF),
                            B("ln2Th0"))
                    w1_half(ring, rw1h[0][0] + 4096, rw1h[0][1], F_g[1], B("g1"),
                            c_rb1[:, 4:8], B("c_rb1"), 0, (pC, pD, pE, pF),
                            B("ln2Th0"))
                stageZ1A(ot)
            _reg(f"s{s}:epi")
            stageZ1V(4)
            stageZ2(4)
            epilogue_post(3)
            routing_mm(2)
            epilogue_pre(4)
            epilogue_post(4)
            routing_mm(3)
            w1_half(ring, rw1h[1][0], rw1h[1][1], F_g[2], B("g2"),
                    c_rb1[:, 8:12], B("c_rb1"), 0, (pC, pD, pE, pF),
                    B("ln2Th0"))
            w1_half(ring, rw1h[1][0] + 4096, rw1h[1][1], F_g[3], B("g3"),
                    c_rb1[:, 12:16], B("c_rb1"), 0, (pC, pD, pE, pF),
                    B("ln2Th0"))
            top2_pair((0, 1, 2, 3))
            tap("z", z_sb[:], [B("z1"), B("z2"), B("z3"), B("z4")], 4096)
            ckpt(5)
            tap("y1", y1[:], [B("y1_1"), B("y1_2"), B("y1_3"), B("y1_4")], 4096)
            tap("ln2T", arn[:, F_ln2T:F_ln2T + 4096], [ln2Tb], 4096)
            tap("cf", cf_sb[:], [B("cf")], 36)
            ckpt(6)
            _reg(f"s{s}:w1sh")
            use_set("silu")

            def w1_block(src, src_off, srcb, dst_off, dstb, bias_ap, biasb):
                for m in range(4):
                    pt = (pA, pB, pD, pE)[m]
                    pre_read(PE, [srcb, ln2Tb])
                    pre_write(PE, [pb(pt)])
                    for k in range(8):
                        lt = src[:, src_off + k * 512 + 128 * m:src_off + k * 512 + 128 * m + 128]
                        rh = arn[:, F_ln2T + k * 512:F_ln2T + k * 512 + 512]
                        i = mm(pt[:], lt, rh, k == 0, k == 7)
                    post(PE, i, writes=[pb(pt)], reads=[srcb, ln2Tb])
                    pre_read(A, [pb(pt), biasb])
                    pre_write(A, [dstb])
                    i = nc.scalar.activation(
                        arn[:, dst_off + 512 * m:dst_off + 512 * m + 512], pt[:],
                        AF.Silu, bias=bias_ap[:, m:m + 1])
                    post(A, i, writes=[dstb], reads=[pb(pt), biasb])
            shb = B("sh")
            w1_half(c_sw1, 0, B("c_sw1"), F_sh, shb, c_sb1, B("c_sb1"), 1,
                    (pA, pB, pD, pE), ln2Tb)
            use_set("silu")
            gb = [B(f"g{e}") for e in range(8)]
            for e in range(8):
                _reg(f"s{s}:w1e{e}")
                pr = e // 2
                if e % 2 == 0 and pr > 0:
                    pass
                eo = (e % 2) * 4096
                rwo, rwb = rw1h[pr]
                if e <= 3:
                    w1_half(ring, rwo + eo, rwb, F_g[e], gb[e],
                            c_rb1[:, 4 * e:4 * e + 4], B("c_rb1"), 1,
                            (pA, pB, pD, pE), ln2Tb)
                else:
                    w1_block(ring, rwo + eo, rwb, F_g[e], gb[e],
                             c_rb1[:, 4 * e:4 * e + 4], B("c_rb1"))
                dcfb = B("dcf")
                pre_read(V, [B("cf"), B("c_eye")])
                pre_write(V, [dcfb])
                for tc in range(4):
                    i = nc.vector.tensor_scalar(
                        arn[:, F_dcf + 128 * tc:F_dcf + 128 * tc + 128], c_eye[:],
                        cf_sb[:, 9 * tc + e:9 * tc + e + 1], 0.0, OP.mult, OP.add)
                post(V, i, writes=[dcfb], reads=[B("cf"), B("c_eye")])
                cbcb = B("cbc")
                pre_read(PE, [dcfb, B("c_ones")])
                pre_write(PE, [pb(pC)])
                for tc in range(4):
                    i = mm(pC[:, 128 * tc:128 * tc + 128], c_ones[:],
                           arn[:, F_dcf + 128 * tc:F_dcf + 128 * tc + 128], True, True)
                post(PE, i, writes=[pb(pC)], reads=[dcfb, B("c_ones")])
                pre_read(V, [pb(pC)])
                pre_write(V, [B("cbc")])
                i = nc.vector.tensor_copy(arn[:, F_cbc:F_cbc + 512], pC[:])
                post(V, i, writes=[B("cbc")], reads=[pb(pC)])
                for m in range(4):
                    pre_read(G, [gb[e], B("cbc")])
                    pre_write(G, [gb[e]])
                    i = nc.gpsimd.tensor_tensor(
                        arn[:, F_g[e] + 512 * m:F_g[e] + 512 * m + 512],
                        arn[:, F_g[e] + 512 * m:F_g[e] + 512 * m + 512],
                        arn[:, F_cbc:F_cbc + 512], OP.mult)
                    post(G, i, writes=[gb[e]], reads=[B("cbc")])
                if e in (3, 5):
                    rw1h[e // 2 + 1] = load_piece(4 + e // 2 + 1)

            tap("sh", arn[:, F_sh:F_sh + 2048], [shb], 2048)
            tap("g0", arn[:, F_g[0]:F_g[0] + 2048], [B("g0")], 2048)
            tap("g7", arn[:, F_g[7]:F_g[7] + 2048], [B("g7")], 2048)
            ckpt(7)
            pre_read(V, [B("cf")])
            pre_write(V, [B("cfb")])
            i = nc.vector.tensor_copy(cfb[:], cf_sb[:])
            post(V, i, writes=[B("cfb")], reads=[B("cf")])
            pre_read(PE, [B("cfb"), B("c_eye")])
            pre_write(PE, [pb(pTR)])
            for tc in range(4):
                i = nc.tensor.transpose(pTR[0:9, 128 * tc:128 * tc + 128],
                                        cfb[:, 9 * tc:9 * tc + 9], c_eye[:])
            post(PE, i, writes=[pb(pTR)], reads=[B("cfb"), B("c_eye")])
            pre_read(V, [pb(pTR)])
            pre_write(V, [B("cfT")])
            i = nc.vector.tensor_copy(cfT[:], pTR[0:9, 0:512])
            post(V, i, writes=[B("cfT")], reads=[pb(pTR)])
            for grp in range(2):
                _reg(f"s{s}:w2g{grp}")
                tcs = (0, 1) if grp == 0 else (2, 3)
                pos = {tcs[0]: (pD, pE), tcs[1]: (pF, pC)}
                started = {}
                for tc in tcs:
                    pl, ph = pos[tc]
                    pre_read(PE, [shb, B("cfT"), B("c_rb2"), B("c_sw2")])
                    pre_write(PE, [pb(pl), pb(ph)])
                    for bk in range(2):
                        pt = pl if bk == 0 else ph
                        for m in range(4):
                            lt = arn[:, F_sh + m * 512 + 128 * tc:F_sh + m * 512 + 128 * tc + 128]
                            rh = c_sw2[:, m * 1024 + 512 * bk:m * 1024 + 512 * bk + 512]
                            i = mm(pt[:], lt, rh, m == 0, False)
                        i = mm(pt[:], cfT[:, 128 * tc:128 * tc + 128],
                               c_rb2[:, 512 * bk:512 * bk + 512], False, False)
                    post(PE, i, writes=[pb(pl), pb(ph)],
                         reads=[shb, B("cfT"), B("c_rb2"), B("c_sw2")])
                for pr in range(4):
                    rw2o, rw2b = load_piece(8 + pr)
                    for tc in tcs:
                        pl, ph = pos[tc]
                        pre_read(PE, [rw2b] + [gb[2 * pr], gb[2 * pr + 1]])
                        pre_write(PE, [pb(pl), pb(ph)])
                        for ei in range(2):
                            e = 2 * pr + ei
                            eo = 4096 * ei
                            for bk in range(2):
                                pt = pl if bk == 0 else ph
                                for m in range(4):
                                    lt = arn[:, F_g[e] + m * 512 + 128 * tc:
                                             F_g[e] + m * 512 + 128 * tc + 128]
                                    rh = ring[:, rw2o + eo + m * 1024 + 512 * bk:
                                              rw2o + eo + m * 1024 + 512 * bk + 512]
                                    last = pr == 3 and ei == 1 and m == 3
                                    i = mm(pt[:], lt, rh, False, last)
                        post(PE, i, writes=[pb(pl), pb(ph)],
                             reads=[rw2b, gb[2 * pr], gb[2 * pr + 1]])
                if grp == 1 and s + 1 < NSTI:
                    _reg(f"s{s}:prolog")
                    STV[s + 1] = prologue(s + 1)
                    _reg(f"s{s}:w2tail")
                _reg(f"s{s}:y1o{grp}")
                for tc in tcs:
                    pl, ph = pos[tc]
                    y1b = B(f"y1_{tc + 1}")
                    pre_read(V, [pb(pl), pb(ph)])
                    pre_write(V, [y1b])
                    nc.vector.tensor_tensor(y1[:, tc * D:tc * D + 512],
                                            y1[:, tc * D:tc * D + 512], pl[:], OP.add)
                    i = nc.vector.tensor_tensor(y1[:, tc * D + 512:(tc + 1) * D],
                                                y1[:, tc * D + 512:(tc + 1) * D],
                                                ph[:], OP.add)
                    post(V, i, writes=[y1b], reads=[pb(pl), pb(ph)])
                    orow = r0 + 128 * tc
                    qO.dma(OUT[orow:orow + 128, :], y1[:, tc * D:(tc + 1) * D],
                           reads=[y1b])


    except _Stop:
        pass
    for q in (qW, qX, qO):
        for ln in q.lanes:
            q.e.wait_ge(ln.h, ln.n)
    for ln in qO.lanes:
        nc.sync.wait_ge(ln.h, ln.n)
        nc.vector.wait_ge(ln.h, ln.n)
    stack.close()
    return nc


def host_in_maps(L0_attn_x, L0_attn_wq, L0_attn_wk, L0_attn_wv, L0_attn_wo,
                 L0_attn_norm_scale, L0_ffn_routing_W, L0_ffn_expert_bias,
                 L0_ffn_sW1, L0_ffn_sb1, L0_ffn_sW2, L0_ffn_sb2,
                 L0_ffn_rW1, L0_ffn_rb1, L0_ffn_rW2, L0_ffn_rb2, ncores=8):
    x = np.asarray(L0_attn_x, dtype=np.float32)
    nsc = np.asarray(L0_attn_norm_scale, np.float32)
    rb1 = np.asarray(L0_ffn_rb1, np.float32)
    sb1 = np.asarray(L0_ffn_sb1, np.float32)
    rb1c = np.empty((128, 32), np.float32)
    for e in range(8):
        for k in range(4):
            rb1c[:, 4 * e + k] = rb1[e, 128 * k:128 * k + 128]
    sb1c = np.empty((128, 4), np.float32)
    for k in range(4):
        sb1c[:, k] = sb1[128 * k:128 * k + 128]
    def tile_w(w, kk):
        w = np.asarray(w, np.float32)
        return np.ascontiguousarray(
            w.reshape(kk, 128, -1).transpose(1, 0, 2).reshape(128, -1))

    shared = dict(
        wq=tile_w(L0_attn_wq, 8),
        wk=tile_w(L0_attn_wk, 8),
        wv=tile_w(L0_attn_wv, 8),
        wo=tile_w(L0_attn_wo, 8),
        sw1=tile_w(L0_ffn_sW1, 8),
        sw2=tile_w(L0_ffn_sW2, 4),
        rw1=np.stack([tile_w(np.asarray(L0_ffn_rW1, np.float32)[e], 8)
                      for e in range(8)]),
        rw2=np.stack([tile_w(np.asarray(L0_ffn_rW2, np.float32)[e], 4)
                      for e in range(8)]),
        routingT=tile_w(np.asarray(L0_ffn_routing_W, np.float32).T, 8),
        rb2sb2=np.ascontiguousarray(
            np.concatenate([np.asarray(L0_ffn_rb2, np.float32),
                            np.asarray(L0_ffn_sb2, np.float32)[None, :]], 0)),
        ebias=np.ascontiguousarray(np.asarray(L0_ffn_expert_bias, np.float32)[None, :]),
        rb1c=rb1c, sb1c=sb1c,
        scale_bc=np.ascontiguousarray(np.tile(nsc[None, :], (128, 1))),
    )
    in_maps = []
    for core in range(ncores):
        b, p0 = core // 4, (core % 4) * TPC
        xc = np.zeros((XR, D), np.float32)
        if p0 > 0:
            xc[64:128] = x[b, p0 - 64:p0]
        xc[128:] = x[b, p0:p0 + TPC]
        m = dict(shared)
        m["x"] = xc
        m.update(_host_consts(p0))
        in_maps.append(m)
    return in_maps


def kernel(**inputs):
    in_maps = host_in_maps(**inputs)
    nc = build_program()
    res = run_bass_kernel_spmd(nc, in_maps, core_ids=list(range(8)))
    out = np.empty((2, 16384, D), np.float32)
    for core in range(8):
        b, p0 = core // 4, (core % 4) * TPC
        out[b, p0:p0 + TPC] = res.results[core]["y"]
    return out
